# version 21
# baseline (speedup 1.0000x reference)
import numpy as np
import ml_dtypes
from contextlib import ExitStack
from collections import Counter

import concourse.bass as bass
import concourse.mybir as mybir
from concourse.bass_utils import run_bass_kernel_spmd

F32 = mybir.dt.float32
BF16 = mybir.dt.bfloat16
ALU = mybir.AluOpType
ACTF = mybir.ActivationFunctionType
AX = mybir.AxisListType.X

D = 1024
DC = D // 128
EPS_LN = 1e-5
EPS_Q = 1e-5
N_CORES = 8
G0 = ("sV", 0)


class Walk:
    def __init__(self, ename, eng, sems):
        self.ename = ename
        self.eng = eng
        self.sems = sems
        self.c = Counter()

    AUTO = {"vector": "sV", "scalar": "sS"}

    def op(self, target, fn, inc=None, wait=None, is_dma=False):
        auto = None if is_dma else self.AUTO.get(target)
        if auto is not None:
            assert inc is None or inc == (auto, 1), f"bad inc {inc} on {target}"
        if wait:
            for sem, val in wait:
                assert val <= self.c[sem], \
                    f"wait {sem}>={val} ahead of counter {self.c[sem]} ({self.ename})"
                if val > 0 and self.ename == target:
                    self.eng.wait_ge(self.sems[sem], val)
        if self.ename == target:
            inst = fn(self.eng)
            if auto is not None:
                if self.c[auto] > 0:
                    inst._wait_ge(self.sems[auto], self.c[auto])
                inst.then_inc(self.sems[auto], 1)
            elif inc is not None:
                inst.then_inc(self.sems[inc[0]], inc[1])
        if auto is not None:
            self.c[auto] += 1
        elif inc is not None:
            self.c[inc[0]] += inc[1]

    def dma(self, target, out, in_, sem, wait=None):
        self_wait = [(sem, self.c[sem])]
        self.op(target, lambda e: e.dma_start(out=out, in_=in_), inc=(sem, 16),
                wait=self_wait + (wait or []), is_dma=True)

    def g(self, sem):
        return (sem, self.c[sem])


def build_program(T=2048, NSEQ=2, F=4096):
    assert T % 512 == 0 and F % 1024 == 0
    FC = F // 128
    FS = F // 512
    NT = T // 128
    GT = min(1024, T)
    TPG = GT // 128
    NG = T // GT
    SC = T // 512

    nc = bass.Bass("TRN2", target_bir_lowering=False, debug=False)

    x_t = nc.dram_tensor("x", [NSEQ, T, D], F32, kind="ExternalInput")
    w1t_t = nc.dram_tensor("w1t", [D, F], F32, kind="ExternalInput")
    w2t_t = nc.dram_tensor("w2t", [F, D], F32, kind="ExternalInput")
    wq_t = nc.dram_tensor("wq", [D, D], F32, kind="ExternalInput")
    wk_t = nc.dram_tensor("wk", [D, D], F32, kind="ExternalInput")
    wv_t = nc.dram_tensor("wv", [D, D], F32, kind="ExternalInput")
    b1r_t = nc.dram_tensor("b1r", [128, FC], F32, kind="ExternalInput")
    b2f_t = nc.dram_tensor("b2f", [1, D], F32, kind="ExternalInput")
    id_t = nc.dram_tensor("id128", [128, 128], BF16, kind="ExternalInput")
    onesf_t = nc.dram_tensor("onesf", [128, 128], F32, kind="ExternalInput")
    onesb_t = nc.dram_tensor("onesb", [1, 512], BF16, kind="ExternalInput")
    mask_t = nc.dram_tensor("mask", [128, 128], F32, kind="ExternalInput")
    out_t = nc.dram_tensor("out", [NSEQ, T, D], F32, kind="ExternalOutput")
    w1q_t = nc.dram_tensor("w1q", [FS, 128, DC, 512], BF16, kind="Internal")
    w2q_t = nc.dram_tensor("w2q", [4, 128, FC, 256], BF16, kind="Internal")

    x = x_t.ap()
    out = out_t.ap()

    ctx = ExitStack()
    SB = lambda n, sh, dt: ctx.enter_context(nc.sbuf_tensor(n, sh, dt))

    stream = SB("stream", [128, NT, D], F32)
    abig = SB("abig", [128, 16 * 1024], F32)
    abig_bf = abig.bitcast(BF16)
    assert FC * GT <= 32768 and 2 * DC * T <= 32768
    h_sb = abig_bf.reshape([128, 32768 // (FC * GT), FC, GT])
    K_view = abig_bf.reshape([128, 32768 // (DC * T), DC, T])
    V_view = abig_bf.reshape([128, 32768 // (NT * D), NT, D])

    amid = SB("amid", [128, 8 * 1024], F32)
    amid_bf = amid.bitcast(BF16)
    assert DC * T <= 16384 and DC * GT <= 8192
    xlng = amid_bf.reshape([128, 16384 // (DC * GT), DC, GT])
    xln2 = amid_bf.reshape([128, 16384 // (DC * T), DC, T])
    qin = amid.reshape([128, 8, 1024])
    qstg = amid_bf.reshape([128, 16, 1024])
    amid_i32 = amid.bitcast(mybir.dt.int32).reshape([128, 8, 1024])
    qin2a = amid.reshape([128, 4, 2048])
    stg2a = amid_bf.reshape([128, 8, 2048])
    stg2b = amid_bf.reshape([128, 8, 2048])

    aw = SB("aw", [128, 8 * 1024], F32)
    aw_bf = aw.bitcast(BF16)
    w1b = aw_bf.reshape([128, 16384 // (DC * 512), DC, 512])
    WA = 8192 // (FC * 256)
    w2s = aw_bf.reshape([128, 16384 // (FC * 256), FC, 256])
    wkb = aw_bf.reshape([128, 2, DC, D])
    pslot = aw_bf.reshape([128, 8, 2048])
    q_view = aw_bf.reshape([128, 4, DC, 512])
    qin2b = aw.reshape([128, 4, 2048])
    i32t2 = aw.bitcast(mybir.dt.int32).reshape([128, 4, 2048])

    xm = SB("xm", [128, 2, D], BF16)
    sqs = SB("sqs", [128, D], BF16)
    ssum = SB("ssum", [128, NT], F32)
    ssq = SB("ssq", [128, NT], F32)
    mneg = SB("mneg", [128, NT], F32)
    ex2e = SB("ex2e", [128, NT], F32)
    m2v = SB("m2v", [128, NT], F32)
    vpe = SB("vpe", [128, NT], F32)
    rcv = SB("rcv", [128, NT], F32)
    rinv = SB("rinv", [128, NT], F32)
    nmr = SB("nmr", [128, NT], F32)
    gp = SB("gp", [128, 32], F32)
    gtot = SB("gtot", [128, 2], F32)
    invg = SB("invg", [128, 2], F32)
    id_sb = SB("id_sb", [128, 128], BF16)
    onesf_sb = SB("onesf_sb", [128, 128], F32)
    onesb_sb = SB("onesb_sb", [1, 512], BF16)
    mask_sb = SB("mask_sb", [128, 128], F32)
    b1r_sb = SB("b1r_sb", [128, FC], F32)
    b2bf = SB("b2bf", [1, D], BF16)
    cmax = SB("cmax", [128, 4], F32)
    negb = SB("negb", [128, 1], F32)
    rs_sb = SB("rs_sb", [128, 4], F32)
    rsum = SB("rsum", [128, 1], F32)
    rinvp = SB("rinvp", [128, 2], F32)
    pbuf2 = SB("pbuf2", [128, 2048], BF16)

    PS = lambda n: ctx.enter_context(nc.psum_tensor(n, [128, 512], F32))
    pA = [PS("pA0"), PS("pA1"), PS("pA2"), PS("pA3")]
    pY = [PS("pY0"), PS("pY1")]
    pT = [PS("pT0"), PS("pT1")]

    sems = {n: ctx.enter_context(nc.semaphore(n))
            for n in ("dG", "sT", "sV", "sS", "dC", "dQa", "dQb", "dQc", "dQd",
                      "dQe", "dQf", "dQg", "dQh", "dSa", "dSb", "dSc", "dSd",
                      "dX", "dW1a", "dW1b", "dW2a", "dW2b", "dO")}

    def walk(w: Walk):
        c = w.c
        gates = {
            "xm": [G0, G0],
            "pt": [G0, G0],
            "pa": [G0, G0, G0, G0],
            "py": [G0, G0],
            "w1": [G0, G0],
            "w2": [G0, G0],
            "h": [G0, G0],
            "aw": [G0],
            "store": [G0, G0],
            "p2v": [G0],
            "w1q": [[G0]] * FS,
            "w2q": [[G0]] * 4,
            "prevexp": [G0],
            "qext": [],
            "xpre": [],
            "premarks": None,
        }

        for dst, src in ((id_sb[:], id_t.ap()), (onesf_sb[:], onesf_t.ap()),
                         (onesb_sb[:], onesb_t.ap()), (mask_sb[:], mask_t.ap()),
                         (b1r_sb[:], b1r_t.ap()), (qin2a[0:1, 2, 0:D], b2f_t.ap())):
            w.dma("sync", dst, src, "dC")
        w.op("vector", lambda e: e.tensor_copy(b2bf[:], qin2a[0:1, 2, 0:D]),
             inc=("sV", 1), wait=[w.g("dC")])

        QSEM2 = [["dQa", "dQb"], ["dQc", "dQd"], ["dQe", "dQf"], ["dQg", "dQh"]]
        SSEM = ["dSa", "dSb"]
        QBUF = [qin2a[:, 2, :], qin2b[:, 2, :]]
        NQB = 2
        if NT >= TPG + 4:
            NQB = 4
            QBUF.append(stream[:, TPG:TPG + 2, :].rearrange("p t d -> p (t d)"))
            QBUF.append(stream[:, TPG + 2:TPG + 4, :].rearrange("p t d -> p (t d)"))
        STG = [stg2a[:, 6, :], stg2b[:, 7, :]]
        T2B = i32t2.bitcast(BF16).reshape([128, 8, 2048])[:, 6, :]
        qin_free = [("sV", 1), G0, G0, G0]
        stg_free = [G0, G0]
        pend_writes = []

        def flush_write():
            if pend_writes:
                dst, ssrc, si, sv, tag = pend_writes.pop(0)
                w.dma("sync", dst, ssrc, SSEM[si], wait=[("sV", sv)])
                stg_free[si] = w.g(SSEM[si])
                if tag is not None:
                    done = [w.g(s) for s in SSEM]
                    nf = CW // 512
                    for fs2 in range(tag * nf, tag * nf + nf):
                        gates["w1q"][fs2] = done

        CW = min(2048, F)
        NFH = F // CW

        def w1_chunks():
            return [(fh * DC + ci, ci, fh)
                    for fh in range(NFH) for ci in range(DC)]

        def load_chunk_halves(mat, ci, fh, k, bi, gate):
            HW_ = CW // 2 if mat == 0 else 1024
            for a in range(2):
                if mat == 0:
                    csrc = w1t_t.ap()[ci * 128:(ci + 1) * 128,
                                      fh * CW + a * HW_:fh * CW + (a + 1) * HW_]
                else:
                    csrc = w2t_t.ap()[k * 256 + a * 128:k * 256 + (a + 1) * 128, :]
                w.dma("sync", QBUF[bi][:, a * HW_:(a + 1) * HW_], csrc,
                      QSEM2[bi][a], wait=[gate] if a == 0 else None)
            return [w.g(QSEM2[bi][0]), w.g(QSEM2[bi][1])]

        def gamma_chunk(mat, k, ci, fh, bi, extra=None):
            qvs = load_chunk_halves(mat, ci, fh, k, bi, qin_free[bi])
            nw = CW // 1024 if mat == 0 else 2
            for a in range(nw):
                ws = [qvs[a if nw > 1 else 0], qvs[1]] if a == 0 or nw > 1 else []
                if a == 0 and extra is not None:
                    ws = ws + [extra]
                w.op("scalar", lambda e, k=k, bi=bi, a=a, nw=nw: e.activation(
                    sqs[:], QBUF[bi][:, a * 1024:(a + 1) * 1024],
                    ACTF.Abs, accum_out=gp[:, nw * k + a:nw * k + a + 1]),
                    wait=ws or None)
            qin_free[bi] = w.g("sS")

        def gamma_finish(mat, nk):
            w.op("tensor", lambda e, nk=nk: e.matmul(
                pT[0][:, 0:nk], onesf_sb[:], gp[:, 0:nk], start=True, stop=True),
                inc=("sT", 1), wait=[w.g("sS"), gates["pt"][0]])
            w.op("vector", lambda e, nk=nk, mat=mat: e.reduce_sum(
                gtot[:, mat:mat + 1], pT[0][:, 0:nk], axis=AX), wait=[w.g("sT")])
            w.op("vector", lambda e, mat=mat: e.tensor_scalar(
                gtot[:, mat:mat + 1], gtot[:, mat:mat + 1], 1.0 / (D * F), EPS_Q,
                ALU.mult, ALU.add))
            w.op("vector", lambda e, mat=mat: e.reciprocal(
                invg[:, mat:mat + 1], gtot[:, mat:mat + 1]))
            gates["pt"][0] = w.g("sV")

        def quant_chunk(mat, k, ci, fh, bi, si):
            qvs = load_chunk_halves(mat, ci, fh, k, bi, qin_free[bi])
            ig = invg[:, mat:mat + 1]
            qv = qvs[1] if True else None
            wd = CW if mat == 0 else 2048
            w.op("vector", lambda e, bi=bi, ig=ig, wd=wd, si=si: e.tensor_scalar(
                STG[si][:, 0:wd], QBUF[bi][:, 0:wd], ig, 0.5,
                ALU.mult, ALU.is_ge), wait=[qvs[0], qvs[1], stg_free[si]])
            w.op("vector", lambda e, bi=bi, ig=ig, wd=wd: e.tensor_scalar(
                T2B[:, 0:wd], QBUF[bi][:, 0:wd], ig, -0.5, ALU.mult, ALU.is_le))
            qin_free[bi] = w.g("sV")
            w.op("vector", lambda e, si=si, wd=wd: e.tensor_tensor(
                STG[si][:, 0:wd], STG[si][:, 0:wd], T2B[:, 0:wd], ALU.subtract))
            if mat == 0:
                nf = CW // 512
                dst = w1q_t.ap()[fh * nf:fh * nf + nf, :, ci, :].rearrange(
                    "f p s -> p f s")
                ssrc = STG[si][:, 0:CW].rearrange("p (f s) -> p f s", f=nf)
                tag = fh if ci == DC - 1 else None
                pend_writes.append((dst, ssrc, si, w.c["sV"], tag))
            else:
                for a in range(2):
                    dst = w2q_t.ap()[0:4, :, 2 * k + a, :].rearrange("q p s -> p q s")
                    ssrc = STG[si][:, a * 1024:(a + 1) * 1024].rearrange(
                        "p (q s) -> p q s", q=4)
                    pend_writes.append((dst, ssrc, si, w.c["sV"], None))
            lim = 1 if mat == 0 else 2
            while len(pend_writes) > lim:
                flush_write()

        def quant_pre():
            for k, ci, fh in w1_chunks():
                gamma_chunk(0, k, ci, fh, k % NQB)
            gamma_finish(0, (CW // 1024) * DC * NFH)
            g1mm = w.g("sT")
            for k in range(F // 256):
                gamma_chunk(1, k, 0, 0, k % NQB, extra=g1mm if k == 0 else None)
            gamma_finish(1, 2 * (F // 256))
            for i, (k, ci, fh) in enumerate(w1_chunks()):
                quant_chunk(0, k, ci, fh, i % NQB, i % 2)
            for k in range(F // 256):
                quant_chunk(1, k, 0, 0, k % NQB, k % 2)
            while pend_writes:
                flush_write()
            qdone = [w.g(s) for s in SSEM] + [w.g("sV")]
            gates["w2q"] = [qdone] * 4
            gates["qext"] = [w.g("sV"), w.g("sS")]

        def ln_stats(tiles, v_waits, s_waits):
            ntl = len(tiles)
            t0 = tiles[0]
            for i, t in enumerate(tiles):
                st_ap = stream[:, t, :]
                w.op("vector", lambda e, st_ap=st_ap, t=t: e.reduce_sum(
                    ssum[:, t:t + 1], st_ap, axis=AX, negate=True),
                    wait=(v_waits if i == 0 else None))
                w.op("scalar", lambda e, st_ap=st_ap, t=t: e.activation(
                    sqs[:], st_ap, ACTF.Square, accum_out=ssq[:, t:t + 1]),
                    wait=(s_waits if i == 0 else None))
            sq_ss = w.g("sS")
            w.op("vector", lambda e, t0=t0, ntl=ntl: e.tensor_scalar(
                mneg[:, t0:t0 + ntl], ssum[:, t0:t0 + ntl], 1.0 / D, None, ALU.mult),
                wait=[sq_ss])
            w.op("vector", lambda e, t0=t0, ntl=ntl: e.tensor_scalar(
                ex2e[:, t0:t0 + ntl], ssq[:, t0:t0 + ntl], 1.0 / D, EPS_LN,
                ALU.mult, ALU.add))
            w.op("vector", lambda e, t0=t0, ntl=ntl: e.tensor_tensor(
                m2v[:, t0:t0 + ntl], mneg[:, t0:t0 + ntl], mneg[:, t0:t0 + ntl],
                ALU.mult))
            w.op("vector", lambda e, t0=t0, ntl=ntl: e.tensor_tensor(
                vpe[:, t0:t0 + ntl], ex2e[:, t0:t0 + ntl], m2v[:, t0:t0 + ntl],
                ALU.subtract))
            w.op("vector", lambda e, t0=t0, ntl=ntl: e.reciprocal(
                rcv[:, t0:t0 + ntl], vpe[:, t0:t0 + ntl]))
            rc_sv = w.g("sV")
            w.op("scalar", lambda e, t0=t0, ntl=ntl: e.activation(
                rinv[:, t0:t0 + ntl], rcv[:, t0:t0 + ntl], ACTF.Sqrt),
                wait=[rc_sv])
            ri_ss = w.g("sS")
            w.op("vector", lambda e, t0=t0, ntl=ntl: e.tensor_tensor(
                nmr[:, t0:t0 + ntl], mneg[:, t0:t0 + ntl], rinv[:, t0:t0 + ntl],
                ALU.mult), wait=[ri_ss])

        def ln_tr(tiles, dest_fn):
            marks = []
            for i, t in enumerate(tiles):
                st_ap = stream[:, t, :]
                xs = i % 2
                w.op("vector", lambda e, st_ap=st_ap, t=t, xs=xs: e.tensor_scalar(
                    xm[:, xs, :], st_ap, rinv[:, t:t + 1], nmr[:, t:t + 1],
                    ALU.mult, ALU.add), wait=[gates["xm"][xs]])
                xm_v = w.g("sV")
                for cb in range(2):
                    if i % 2 == 0:
                        bank = pT[cb]
                        glist, gi = gates["pt"], cb
                    else:
                        bank = pA[cb]
                        glist, gi = gates["pa"], cb
                    for cc in range(4):
                        ch = cb * 4 + cc
                        w.op("tensor", lambda e, xs=xs, ch=ch, cc=cc, bank=bank: e.matmul(
                            bank[:, cc * 128:(cc + 1) * 128],
                            xm[:, xs, ch * 128:(ch + 1) * 128], id_sb[:],
                            start=True, stop=True),
                            inc=("sT", 1) if cc == 3 else None,
                            wait=[xm_v, glist[gi]] if cc == 0 else None)
                    dst = dest_fn(i, t, cb)
                    w.op("scalar", lambda e, dst=dst, bank=bank: e.copy(dst, bank[:]),
                         inc=("sS", 1), wait=[w.g("sT")])
                    glist[gi] = w.g("sS")
                gates["xm"][xs] = w.g("sT")
                marks.append((w.g("sV"), w.g("sS")))
            return marks

        def load_x_group(s, g):
            src = x[s, g * GT:(g + 1) * GT, :].rearrange("(t p) d -> p t d", p=128)
            w.dma("sync", stream[:, g * TPG:(g + 1) * TPG, :], src, "dX",
                  wait=[gates["store"][g % 2]] + (gates["qext"] if g >= 1 else []))
            return w.g("dX")

        def ffn_dest(i, t, cb):
            tl = t % TPG
            return xlng[:, 0, cb * 4:cb * 4 + 4, tl * 128:(tl + 1) * 128]

        def ffn_phase(s, is_p1, pre_marks=None, fs_hook=None, tr_hook=None):
            for g in range(NG):
                wq_queue = "gpsimd" if fs_hook is not None and g == 0 else "sync"
                tiles = list(range(g * TPG, (g + 1) * TPG))
                if g == 0 and pre_marks is None and is_p1:
                    if gates.get("xpre"):
                        xw = gates["xpre"][0]
                        gates["xpre"] = []
                    else:
                        xw = load_x_group(s, 0)
                    ln_stats(tiles, [xw], [xw, ("sV", 1)])
                if g == 0 and pre_marks is not None:
                    marks = pre_marks
                else:
                    marks = ln_tr(tiles, ffn_dest)
                xln_rdy = marks[-1]

                NH = GT // 512
                for fs in range(FS):
                    sl = fs % 2
                    wsem = "dG" if wq_queue == "gpsimd" else (
                        "dW1a" if sl == 0 else "dW1b")
                    if fs_hook is not None and g == 0 and fs == max(0, FS - 4):
                        fs_hook()
                    w.dma(wq_queue, w1b[:, sl], w1q_t.ap()[fs], wsem,
                          wait=[gates["w1"][sl]] + gates["w1q"][fs])
                    wv = w.g(wsem)
                    for fj in range(4):
                        fc = fs * 4 + fj
                        par = fc % 3
                        if par == 0:
                            banks, glist, g0i = (pA[0], pA[1]), gates["pa"], 0
                        elif par == 1:
                            banks, glist, g0i = (pA[2], pA[3]), gates["pa"], 2
                        else:
                            banks, glist, g0i = (pT[0], pT[1]), gates["pt"], 0
                        for ci in range(DC):
                            stat = w1b[:, sl, ci, fj * 128:(fj + 1) * 128]
                            for hf in range(NH):
                                bk = banks[hf]
                                w.op("tensor", lambda e, stat=stat, bk=bk, ci=ci, hf=hf: e.matmul(
                                    bk[:], stat, xlng[:, 0, ci, hf * 512:(hf + 1) * 512],
                                    start=(ci == 0), stop=(ci == DC - 1)),
                                    wait=[wv, xln_rdy[0], xln_rdy[1],
                                          glist[g0i], glist[g0i + 1]]
                                    if (ci == 0 and hf == 0) else None,
                                    inc=("sT", 1) if ci == DC - 1 else None)
                        stv = w.g("sT")
                        if fs_hook is not None and g == 0:
                            use_v = False
                        else:
                            use_v = (fc % 2 == 0)
                        eng = "vector" if use_v else "scalar"
                        sem = "sV" if use_v else "sS"
                        for hf in range(NH):
                            bk = banks[hf]
                            if eng == "vector":
                                w.op(eng, lambda e, fc=fc, hf=hf, bk=bk: e.tensor_scalar(
                                    h_sb[:, 0, fc, hf * 512:(hf + 1) * 512], bk[:],
                                    b1r_sb[:, fc:fc + 1], 0.0, ALU.add, ALU.max),
                                    inc=(sem, 1), wait=[stv] if hf == 0 else None)
                            else:
                                w.op(eng, lambda e, fc=fc, hf=hf, bk=bk: e.activation(
                                    h_sb[:, 0, fc, hf * 512:(hf + 1) * 512], bk[:],
                                    ACTF.Relu, bias=b1r_sb[:, fc:fc + 1]),
                                    inc=(sem, 1), wait=[stv] if hf == 0 else None)
                        glist[g0i] = w.g(sem)
                        glist[g0i + 1] = w.g(sem)
                        gates["h"][fc % 2] = w.g(sem)
                    gates["w1"][sl] = w.g("sT")
                h_rdy = (gates["h"][0], gates["h"][1])

                if g + 1 < NG:
                    ntiles = list(range((g + 1) * TPG, (g + 2) * TPG))
                    if is_p1:
                        xw2 = load_x_group(s, g + 1)
                        ln_stats(ntiles, [xw2], [xw2])
                    else:
                        ln_stats(ntiles, None, None)
                if tr_hook is not None and g == NG - 1:
                    tr_hook()

                for dq in range(4):
                    if dq % 2 == 0:
                        buf = w2s[:, WA]
                        bg = [gates["w2"][0]]
                    else:
                        buf = w2s[:, 0]
                        bg = [gates["w2"][1], gates["w1"][0], gates["w1"][1]]
                    wsem = "dG" if wq_queue == "gpsimd" else (
                        "dW2a" if dq % 2 == 0 else "dW2b")
                    w.dma(wq_queue, buf, w2q_t.ap()[dq], wsem,
                          wait=bg + gates["w2q"][dq])
                    wv = w.g(wsem)
                    for tb in range(TPG):
                        bank = pY[tb % 2]
                        t_glob = g * TPG + tb
                        for fc in range(FC):
                            stat = h_sb[:, 0, fc, tb * 128:(tb + 1) * 128]
                            w.op("tensor", lambda e, stat=stat, bank=bank, buf=buf, fc=fc: e.matmul(
                                bank[:, 0:256], stat, buf[:, fc, :],
                                start=(fc == 0), stop=False),
                                wait=[wv, h_rdy[0], h_rdy[1], gates["py"][tb % 2]]
                                if fc == 0 else None)
                        w.op("tensor", lambda e, bank=bank, dq=dq: e.matmul(
                            bank[:, 0:256], onesb_sb[0:1, 0:128],
                            b2bf[0:1, dq * 256:(dq + 1) * 256], start=False, stop=True),
                            inc=("sT", 1))
                        w.op("vector",
                             lambda e, bank=bank, t_glob=t_glob, dq=dq: e.scalar_tensor_tensor(
                                 stream[:, t_glob, dq * 256:(dq + 1) * 256], bank[:, 0:256],
                                 0.5, stream[:, t_glob, dq * 256:(dq + 1) * 256],
                                 ALU.mult, ALU.add),
                             inc=("sV", 1), wait=[w.g("sT")])
                        gates["py"][tb % 2] = w.g("sV")
                    gates["w2"][dq % 2] = w.g("sT")
                gates["aw"][0] = w.g("sT")
                gates["w1"] = [w.g("sT"), w.g("sT")]

                if is_p1:
                    ln_stats(tiles, [("sV", c["sV"])], [("sV", c["sV"])])
                if not is_p1:
                    dst = out[s, g * GT:(g + 1) * GT, :].rearrange("(t p) d -> p t d", p=128)
                    w.dma("sync", dst, stream[:, g * TPG:(g + 1) * TPG, :], "dO",
                          wait=[w.g("sV")])
                    gates["store"][g % 2] = w.g("dO")
                    if g == 0 and s + 1 < NSEQ:
                        gates["xpre"] = [load_x_group(s + 1, 0)]
                    if g == NG - 1 and s + 1 < NSEQ:
                        xw3 = gates["xpre"][0] if gates["xpre"] else G0
                        gates["xpre"] = []
                        ln_stats(list(range(TPG)), [xw3], [xw3])
                        gates["premarks"] = ln_tr(list(range(TPG)), ffn_dest)

        def p2_dest(i, t, cb):
            return xln2[:, 0, cb * 4:cb * 4 + 4, t * 128:(t + 1) * 128]

        p2pre = {"marks": None}

        def p2_tr_hook():
            p2pre["marks"] = ln_tr(list(range(TPG)), p2_dest)

        def attn_phase(s):
            pm = p2pre["marks"] if p2pre["marks"] is not None else []
            p2pre["marks"] = None
            p2marks = pm + ln_tr(list(range(len(pm), NT)), p2_dest)

            w.dma("gpsimd", wkb[:, 0], wk_t.ap().rearrange("(c p) d -> p c d", p=128),
                  "dG", wait=[gates["aw"][0]])
            kv_dg = w.g("dG")
            w.dma("gpsimd", wkb[:, 1], wv_t.ap().rearrange("(c p) d -> p c d", p=128),
                  "dG", wait=[gates["aw"][0]])
            wv_dg = w.g("dG")

            for co in range(DC):
                for ch in range(SC):
                    bi = (co * SC + ch) % 4
                    bank = pA[bi]
                    mk = p2marks[min(ch * 4 + 3, NT - 1)]
                    for ci in range(DC):
                        w.op("tensor", lambda e, co=co, ch=ch, ci=ci, bank=bank: e.matmul(
                            bank[:], wkb[:, 0, ci, co * 128:(co + 1) * 128],
                            xln2[:, 0, ci, ch * 512:(ch + 1) * 512],
                            start=(ci == 0), stop=(ci == DC - 1)),
                            wait=[kv_dg, mk[0], mk[1],
                                  gates["pa"][bi]] if ci == 0 else None,
                            inc=("sT", 1) if ci == DC - 1 else None)
                    w.op("vector", lambda e, co=co, ch=ch, bank=bank: e.tensor_copy(
                        K_view[:, 0, co, ch * 512:(ch + 1) * 512], bank[:]),
                        inc=("sV", 1), wait=[w.g("sT")])
                    gates["pa"][bi] = w.g("sV")

            w.dma("gpsimd", wkb[:, 0], wq_t.ap().rearrange("(c p) d -> p c d", p=128),
                  "dG", wait=[w.g("sT")])
            wq_dg = w.g("dG")

            for tb in range(NT):
                for dh in range(2):
                    bi = (tb * 2 + dh) % 4
                    bank = pA[bi]
                    mk = p2marks[tb]
                    for ci in range(DC):
                        w.op("tensor", lambda e, tb=tb, dh=dh, ci=ci, bank=bank: e.matmul(
                            bank[:], xln2[:, 0, ci, tb * 128:(tb + 1) * 128],
                            wkb[:, 1, ci, dh * 512:(dh + 1) * 512],
                            start=(ci == 0), stop=(ci == DC - 1)),
                            wait=[gates["pa"][bi], wv_dg, mk[0], mk[1]]
                            if ci == 0 else None,
                            inc=("sT", 1) if ci == DC - 1 else None)
                    w.op("scalar", lambda e, tb=tb, dh=dh, bank=bank: e.copy(
                        V_view[:, 1, tb, dh * 512:(dh + 1) * 512], bank[:]),
                        inc=("sS", 1), wait=[w.g("sT")])
                    gates["pa"][bi] = w.g("sS")

            tile_st = {}

            def emit_scores(qt):
                qc = qt // 4
                if qt % 4 == 0:
                    for co in range(DC):
                        bi = co % 4
                        bank = pA[bi]
                        for ci in range(DC):
                            w.op("tensor", lambda e, co=co, ci=ci, qc=qc, bank=bank: e.matmul(
                                bank[:], wkb[:, 0, ci, co * 128:(co + 1) * 128],
                                xln2[:, 0, ci, qc * 512:(qc + 1) * 512],
                                start=(ci == 0), stop=(ci == DC - 1)),
                                wait=[wq_dg, gates["pa"][bi]] if ci == 0 else None,
                                inc=("sT", 1) if ci == DC - 1 else None)
                        w.op("vector", lambda e, co=co, bank=bank: e.tensor_copy(
                            q_view[:, 3, co, :], bank[:]),
                            inc=("sV", 1), wait=[w.g("sT")])
                        gates["pa"][bi] = w.g("sV")
                q_sv = w.g("sV")
                L = (qt + 1) * 128
                nch = (L + 511) // 512
                order = [nch - 1] + list(range(nch - 1))
                for ch in order:
                    W_ = min(512, L - ch * 512)
                    bank = pA[ch]
                    for ci in range(DC):
                        w.op("tensor", lambda e, ch=ch, ci=ci, W_=W_, qt=qt, bank=bank: e.matmul(
                            bank[:, 0:W_],
                            q_view[:, 3, ci, (qt % 4) * 128:(qt % 4 + 1) * 128],
                            K_view[:, 0, ci, ch * 512:ch * 512 + W_],
                            start=(ci == 0), stop=(ci == DC - 1)),
                            wait=[q_sv, gates["pa"][ch]] if ci == 0 else None,
                            inc=("sT", 1) if ci == DC - 1 else None)
                    if ch == nch - 1:
                        lw = L - ch * 512
                        w.op("vector", lambda e, ch=ch, lw=lw: e.tensor_tensor(
                            pA[ch][:, lw - 128:lw], pA[ch][:, lw - 128:lw],
                            mask_sb[:], ALU.add),
                            wait=[w.g("sT"), gates["prevexp"][0]])
                    W2_ = min(512, L - ch * 512)
                    w.op("vector", lambda e, ch=ch, W2_=W2_: e.reduce_max(
                        cmax[:, ch:ch + 1], pA[ch][:, 0:W2_], axis=AX),
                        wait=[w.g("sT"), gates["prevexp"][0]] if ch != nch - 1 else None)
                w.op("vector", lambda e, nch=nch: e.reduce_max(
                    negb[:], cmax[:, 0:nch], axis=AX, negate=True))
                w.op("vector", lambda e: e.tensor_scalar(
                    negb[:], negb[:], 1.0 / 32.0, None, ALU.mult), inc=("sV", 1))
                mx_sv = w.g("sV")
                pb = pslot[:, 4, :] if qt % 2 == 0 else pbuf2
                exp_marks = []
                for ch in range(nch):
                    W_ = min(512, L - ch * 512)
                    w.op("scalar", lambda e, ch=ch, W_=W_, pb=pb: e.activation(
                        pb[ch * 512:ch * 512 + W_] if False else pb[:, ch * 512:ch * 512 + W_],
                        pA[ch][:, 0:W_], ACTF.Exp,
                        bias=negb[:, 0:1], scale=1.0 / 32.0,
                        accum_out=rs_sb[:, ch:ch + 1]),
                        inc=("sS", 1), wait=[mx_sv] if ch == 0 else None)
                    gates["pa"][ch] = w.g("sS")
                    exp_marks.append(w.g("sS"))
                gates["prevexp"][0] = w.g("sS")
                w.op("vector", lambda e, nch=nch: e.reduce_sum(
                    rsum[:], rs_sb[:, 0:nch], axis=AX), wait=[w.g("sS")])
                w.op("vector", lambda e, qt=qt: e.reciprocal(
                    rinvp[:, qt % 2:qt % 2 + 1], rsum[:]), inc=("sV", 1))
                tile_st[qt] = (exp_marks, w.g("sV"), pb)

            def emit_av(qt):
                exp_marks, ri_sv, pb = tile_st.pop(qt)
                nj = qt + 1
                for jb in range((nj + 3) // 4):
                    bank = pT[jb % 2]
                    njb = min(4, nj - jb * 4)
                    for jj in range(njb):
                        j = jb * 4 + jj
                        w.op("tensor", lambda e, j=j, jj=jj, bank=bank, pb=pb: e.matmul(
                            bank[:, jj * 128:(jj + 1) * 128],
                            pb[:, j * 128:(j + 1) * 128], id_sb[:],
                            start=True, stop=True),
                            wait=[exp_marks[jb], gates["pt"][jb % 2]] if jj == 0 else None,
                            inc=("sT", 1) if jj == njb - 1 else None)
                    w.op("scalar", lambda e, jb=jb, njb=njb, bank=bank: e.copy(
                        pslot[:, 5, jb * 512:jb * 512 + njb * 128], bank[:, 0:njb * 128]),
                        inc=("sS", 1), wait=[w.g("sT")])
                    gates["pt"][jb % 2] = w.g("sS")
                    aT_ss = w.g("sS")
                    for jj in range(njb):
                        j = jb * 4 + jj
                        for dh in range(2):
                            w.op("tensor", lambda e, j=j, dh=dh, qt=qt: e.matmul(
                                pY[dh][:], pslot[:, 5, j * 128:(j + 1) * 128],
                                V_view[:, 1, j, dh * 512:(dh + 1) * 512],
                                start=(j == 0), stop=(j == qt)),
                                wait=[aT_ss, gates["py"][0], gates["py"][1]]
                                if (jj == 0 and dh == 0) else None,
                                inc=("sT", 1) if (j == qt and dh == 1) else None)
                yt = w.g("sT")
                for dh in range(2):
                    w.op("vector", lambda e, dh=dh, qt=qt: e.scalar_tensor_tensor(
                        stream[:, qt, dh * 512:(dh + 1) * 512], pY[dh][:],
                        rinvp[:, qt % 2:qt % 2 + 1],
                        stream[:, qt, dh * 512:(dh + 1) * 512],
                        ALU.mult, ALU.add),
                        inc=("sV", 1), wait=[yt, ri_sv] if dh == 0 else None)
                    gates["py"][dh] = w.g("sV")
                if qt == TPG - 1:
                    gv = w.g("sV")
                    ln_stats(list(range(TPG)), [gv], [gv])

            for qt in range(NT):
                emit_scores(qt)
                if qt > 0:
                    emit_av(qt - 1)
            emit_av(NT - 1)
            gates["p2v"][0] = w.g("sV")
            gates["aw"][0] = w.g("sT")
            gates["w1"] = [w.g("sT"), w.g("sT")]
            gates["w2"] = [w.g("sT"), w.g("sT")]

        xw0 = load_x_group(0, 0)
        ln_stats(list(range(TPG)), [xw0], [xw0, ("sV", 1)])
        pre_marks = ln_tr(list(range(TPG)), ffn_dest)
        quant_pre()
        for s in range(NSEQ):
            if s == 0:
                pm = pre_marks
            else:
                pm = gates["premarks"]
                gates["premarks"] = None
            ffn_phase(s, True, pre_marks=pm,
                      fs_hook=(lambda: None) if s == 0 else None,
                      tr_hook=p2_tr_hook if NG > 1 else None)
            attn_phase(s)
            ffn_phase(s, False)

        w.op("sync", lambda e: e.wait_ge(sems["dO"], c["dO"]))
        w.op("gpsimd", lambda e: e.wait_ge(sems["dG"], c["dG"]))

    with nc.Block() as block:
        for ename, dec in (("sync", block.sync), ("gpsimd", block.gpsimd),
                           ("vector", block.vector), ("scalar", block.scalar),
                           ("tensor", block.tensor)):
            def mk(en):
                def f(eng):
                    walk(Walk(en, eng, sems))
                return f
            dec(mk(ename))
    ctx.close()
    return nc


def make_in_maps(L0_attn_x, L0_attn_w_q, L0_attn_w_k, L0_attn_w_v,
                 L0_ffn_W1, L0_ffn_b1, L0_ffn_W2, L0_ffn_b2,
                 NSEQ=2, F=4096, n_cores=N_CORES):
    FC = F // 128
    id128 = np.eye(128, dtype=ml_dtypes.bfloat16)
    onesf = np.ones((128, 128), np.float32)
    onesb = np.ones((1, 512), ml_dtypes.bfloat16)
    mask = np.where(np.arange(128)[None, :] <= np.arange(128)[:, None],
                    np.float32(0.0), np.float32(-1e30)).astype(np.float32)
    w1t = np.ascontiguousarray(np.asarray(L0_ffn_W1, np.float32).T)
    w2t = np.ascontiguousarray(np.asarray(L0_ffn_W2, np.float32).T)
    b1r = np.ascontiguousarray(np.asarray(L0_ffn_b1, np.float32).reshape(FC, 128).T)
    b2f = np.ascontiguousarray(np.asarray(L0_ffn_b2, np.float32).reshape(1, D))
    xfull = np.asarray(L0_attn_x, np.float32)
    in_maps = []
    for core in range(n_cores):
        xs = np.ascontiguousarray(xfull[core * NSEQ:(core + 1) * NSEQ])
        in_maps.append({
            "x": xs, "w1t": w1t, "w2t": w2t,
            "wq": np.ascontiguousarray(np.asarray(L0_attn_w_q, np.float32)),
            "wk": np.ascontiguousarray(np.asarray(L0_attn_w_k, np.float32)),
            "wv": np.ascontiguousarray(np.asarray(L0_attn_w_v, np.float32)),
            "b1r": b1r, "b2f": b2f,
            "id128": id128, "onesf": onesf, "onesb": onesb, "mask": mask,
        })
    return in_maps


def kernel(L0_attn_x, L0_attn_w_q, L0_attn_w_k, L0_attn_w_v,
           L0_ffn_W1, L0_ffn_b1, L0_ffn_W2, L0_ffn_b2):
    T, NSEQ, F = 2048, 2, 4096
    nc = build_program(T=T, NSEQ=NSEQ, F=F)
    in_maps = make_in_maps(L0_attn_x, L0_attn_w_q, L0_attn_w_k, L0_attn_w_v,
                           L0_ffn_W1, L0_ffn_b1, L0_ffn_W2, L0_ffn_b2,
                           NSEQ=NSEQ, F=F)
    res = run_bass_kernel_spmd(nc, in_maps, core_ids=list(range(N_CORES)))
    outs = [np.asarray(r["out"]) for r in res.results]
    return np.concatenate(outs, axis=0).reshape(16, 2048, 1024)


# version 22
# speedup vs baseline: 1.0117x; 1.0117x over previous
import numpy as np
import ml_dtypes
from contextlib import ExitStack
from collections import Counter

import concourse.bass as bass
import concourse.mybir as mybir
from concourse.bass_utils import run_bass_kernel_spmd

F32 = mybir.dt.float32
BF16 = mybir.dt.bfloat16
ALU = mybir.AluOpType
ACTF = mybir.ActivationFunctionType
AX = mybir.AxisListType.X

D = 1024
DC = D // 128
EPS_LN = 1e-5
EPS_Q = 1e-5
N_CORES = 8
G0 = ("sV", 0)


class Walk:
    def __init__(self, ename, eng, sems):
        self.ename = ename
        self.eng = eng
        self.sems = sems
        self.c = Counter()

    AUTO = {"vector": "sV", "scalar": "sS"}

    def op(self, target, fn, inc=None, wait=None, is_dma=False):
        auto = None if is_dma else self.AUTO.get(target)
        if auto is not None:
            assert inc is None or inc == (auto, 1), f"bad inc {inc} on {target}"
        if wait:
            for sem, val in wait:
                assert val <= self.c[sem], \
                    f"wait {sem}>={val} ahead of counter {self.c[sem]} ({self.ename})"
                if val > 0 and self.ename == target:
                    self.eng.wait_ge(self.sems[sem], val)
        if self.ename == target:
            inst = fn(self.eng)
            if auto is not None:
                if self.c[auto] > 0:
                    inst._wait_ge(self.sems[auto], self.c[auto])
                inst.then_inc(self.sems[auto], 1)
            elif inc is not None:
                inst.then_inc(self.sems[inc[0]], inc[1])
        if auto is not None:
            self.c[auto] += 1
        elif inc is not None:
            self.c[inc[0]] += inc[1]

    def dma(self, target, out, in_, sem, wait=None):
        self_wait = [(sem, self.c[sem])]
        self.op(target, lambda e: e.dma_start(out=out, in_=in_), inc=(sem, 16),
                wait=self_wait + (wait or []), is_dma=True)

    def g(self, sem):
        return (sem, self.c[sem])


def build_program(T=2048, NSEQ=2, F=4096):
    assert T % 512 == 0 and F % 1024 == 0
    FC = F // 128
    FS = F // 512
    NT = T // 128
    GT = min(1024, T)
    TPG = GT // 128
    NG = T // GT
    SC = T // 512

    nc = bass.Bass("TRN2", target_bir_lowering=False, debug=False)

    x_t = nc.dram_tensor("x", [NSEQ, T, D], F32, kind="ExternalInput")
    w1t_t = nc.dram_tensor("w1t", [D, F], F32, kind="ExternalInput")
    w2t_t = nc.dram_tensor("w2t", [F, D], F32, kind="ExternalInput")
    wq_t = nc.dram_tensor("wq", [D, D], F32, kind="ExternalInput")
    wk_t = nc.dram_tensor("wk", [D, D], F32, kind="ExternalInput")
    wv_t = nc.dram_tensor("wv", [D, D], F32, kind="ExternalInput")
    b1r_t = nc.dram_tensor("b1r", [128, FC], F32, kind="ExternalInput")
    b2f_t = nc.dram_tensor("b2f", [1, D], F32, kind="ExternalInput")
    id_t = nc.dram_tensor("id128", [128, 128], BF16, kind="ExternalInput")
    onesf_t = nc.dram_tensor("onesf", [128, 128], F32, kind="ExternalInput")
    onesb_t = nc.dram_tensor("onesb", [1, 512], BF16, kind="ExternalInput")
    mask_t = nc.dram_tensor("mask", [128, 128], F32, kind="ExternalInput")
    out_t = nc.dram_tensor("out", [NSEQ, T, D], F32, kind="ExternalOutput")
    w1q_t = nc.dram_tensor("w1q", [FS, 128, DC, 512], BF16, kind="Internal")
    w2q_t = nc.dram_tensor("w2q", [4, 128, FC, 256], BF16, kind="Internal")

    x = x_t.ap()
    out = out_t.ap()

    ctx = ExitStack()
    SB = lambda n, sh, dt: ctx.enter_context(nc.sbuf_tensor(n, sh, dt))

    stream = SB("stream", [128, NT, D], F32)
    abig = SB("abig", [128, 16 * 1024], F32)
    abig_bf = abig.bitcast(BF16)
    assert FC * GT <= 32768 and 2 * DC * T <= 32768
    h_sb = abig_bf.reshape([128, 32768 // (FC * GT), FC, GT])
    K_view = abig_bf.reshape([128, 32768 // (DC * T), DC, T])
    V_view = abig_bf.reshape([128, 32768 // (NT * D), NT, D])

    amid = SB("amid", [128, 8 * 1024], F32)
    amid_bf = amid.bitcast(BF16)
    assert DC * T <= 16384 and DC * GT <= 8192
    xlng = amid_bf.reshape([128, 16384 // (DC * GT), DC, GT])
    xln2 = amid_bf.reshape([128, 16384 // (DC * T), DC, T])
    qin = amid.reshape([128, 8, 1024])
    qstg = amid_bf.reshape([128, 16, 1024])
    amid_i32 = amid.bitcast(mybir.dt.int32).reshape([128, 8, 1024])
    qin2a = amid.reshape([128, 4, 2048])
    stg2a = amid_bf.reshape([128, 8, 2048])
    stg2b = amid_bf.reshape([128, 8, 2048])

    aw = SB("aw", [128, 8 * 1024], F32)
    aw_bf = aw.bitcast(BF16)
    w1b = aw_bf.reshape([128, 16384 // (DC * 512), DC, 512])
    WA = 8192 // (FC * 256)
    w2s = aw_bf.reshape([128, 16384 // (FC * 256), FC, 256])
    wkb = aw_bf.reshape([128, 2, DC, D])
    pslot = aw_bf.reshape([128, 8, 2048])
    q_view = aw_bf.reshape([128, 4, DC, 512])
    qin2b = aw.reshape([128, 4, 2048])
    i32t2 = aw.bitcast(mybir.dt.int32).reshape([128, 4, 2048])

    xm = SB("xm", [128, 2, D], BF16)
    sqs = SB("sqs", [128, D], BF16)
    ssum = SB("ssum", [128, NT], F32)
    ssq = SB("ssq", [128, NT], F32)
    mneg = SB("mneg", [128, NT], F32)
    ex2e = SB("ex2e", [128, NT], F32)
    m2v = SB("m2v", [128, NT], F32)
    vpe = SB("vpe", [128, NT], F32)
    rcv = SB("rcv", [128, NT], F32)
    rinv = SB("rinv", [128, NT], F32)
    nmr = SB("nmr", [128, NT], F32)
    gp = SB("gp", [128, 32], F32)
    gtot = SB("gtot", [128, 2], F32)
    invg = SB("invg", [128, 2], F32)
    id_sb = SB("id_sb", [128, 128], BF16)
    onesf_sb = SB("onesf_sb", [128, 128], F32)
    onesb_sb = SB("onesb_sb", [1, 512], BF16)
    mask_sb = SB("mask_sb", [128, 128], F32)
    b1r_sb = SB("b1r_sb", [128, FC], F32)
    b2bf = SB("b2bf", [1, D], BF16)
    cmax = SB("cmax", [128, 4], F32)
    negb = SB("negb", [128, 1], F32)
    rs_sb = SB("rs_sb", [128, 4], F32)
    rsum = SB("rsum", [128, 1], F32)
    rinvp = SB("rinvp", [128, 2], F32)
    pbuf2 = SB("pbuf2", [128, 2048], BF16)

    PS = lambda n: ctx.enter_context(nc.psum_tensor(n, [128, 512], F32))
    pA = [PS("pA0"), PS("pA1"), PS("pA2"), PS("pA3")]
    pY = [PS("pY0"), PS("pY1")]
    pT = [PS("pT0"), PS("pT1")]

    sems = {n: ctx.enter_context(nc.semaphore(n))
            for n in ("dG", "sT", "sV", "sS", "dC", "dQa", "dQb", "dQc", "dQd",
                      "dQe", "dQf", "dQg", "dQh", "dSa", "dSb", "dSc", "dSd",
                      "dX", "dW1a", "dW1b", "dW2a", "dW2b", "dO")}

    def walk(w: Walk):
        c = w.c
        gates = {
            "xm": [G0, G0],
            "pt": [G0, G0],
            "pa": [G0, G0, G0, G0],
            "py": [G0, G0],
            "w1": [G0, G0],
            "w2": [G0, G0],
            "h": [G0, G0],
            "aw": [G0],
            "store": [G0, G0],
            "p2v": [G0],
            "w1q": [[G0]] * FS,
            "w2q": [[G0]] * 4,
            "prevexp": [G0],
            "qext": [],
            "xpre": [],
            "aw2": [],
            "premarks": None,
        }

        for dst, src in ((id_sb[:], id_t.ap()), (onesf_sb[:], onesf_t.ap()),
                         (onesb_sb[:], onesb_t.ap()), (mask_sb[:], mask_t.ap()),
                         (b1r_sb[:], b1r_t.ap()), (qin2a[0:1, 2, 0:D], b2f_t.ap())):
            w.dma("sync", dst, src, "dC")
        w.op("vector", lambda e: e.tensor_copy(b2bf[:], qin2a[0:1, 2, 0:D]),
             inc=("sV", 1), wait=[w.g("dC")])

        QSEM = ["dQa", "dQb", "dQc", "dQd"]
        QBUF = [qin2a[:, 2, :], qin2b[:, 2, :]]
        NQB = 2
        if NT >= TPG + 4:
            NQB = 4
            QBUF.append(stream[:, TPG:TPG + 2, :].rearrange("p t d -> p (t d)"))
            QBUF.append(stream[:, TPG + 2:TPG + 4, :].rearrange("p t d -> p (t d)"))
        STG = [stg2a[:, 6, :], stg2b[:, 7, :]]
        qin_free = [("sV", 1), G0, G0, G0]
        SSEM = ["dSa", "dSb", "dSc", "dSd"]
        pend_writes = []
        NW2R = 8
        NFQ2 = F // 1024
        w1mark = [G0] * FS
        w2mark = [G0] * 4
        w1slot_free = [[G0], [G0]]
        w2slot_free = [[G0], [G0]]
        stage1_done = [G0] * (FS + 1)
        stage2_done = [G0] * 5
        ssem_i = [0]

        def flush_write():
            if pend_writes:
                dst, ssrc, sv, tag = pend_writes.pop(0)
                si = ssem_i[0] % 4
                ssem_i[0] += 1
                w.dma("sync", dst, ssrc, SSEM[si], wait=[sv])
                if tag is not None:
                    done = [w.g(s) for s in SSEM]
                    kind, idx = tag
                    if kind == 0:
                        gates["w1q"][idx] = done
                        w1slot_free[idx % 2] = done
                    else:
                        gates["w2q"][idx] = done
                        w2slot_free[idx % 2] = done

        def q_load(src_ap, bi, k, extra=None):
            w.dma("sync", QBUF[bi].rearrange("p (c s) -> p c s", s=256),
                  src_ap, QSEM[bi], wait=[qin_free[bi]] + (extra or []))
            return w.g(QSEM[bi])

        def w1_src(fs, h):
            return w1t_t.ap()[:, fs * 512 + h * 256:fs * 512 + (h + 1) * 256]\
                .rearrange("(c p) s -> p c s", p=128)

        def w2_src(dq, fq):
            return w2t_t.ap()[fq * 1024:(fq + 1) * 1024,
                              dq * 256:(dq + 1) * 256]\
                .rearrange("(c p) s -> p c s", p=128)

        gcount = [0]

        def gamma_chunk(mat, src_ap, bi, extra=None):
            qv = q_load(src_ap, bi, 0)
            k = gcount[0]
            gcount[0] += 1
            ws = [qv] + ([extra] if extra is not None else [])
            w.op("scalar", lambda e, k=k, bi=bi: e.activation(
                sqs[:], QBUF[bi][:, 0:1024], ACTF.Abs,
                accum_out=gp[:, 2 * k:2 * k + 1]), wait=ws)
            w.op("scalar", lambda e, k=k, bi=bi: e.activation(
                sqs[:], QBUF[bi][:, 1024:2048], ACTF.Abs,
                accum_out=gp[:, 2 * k + 1:2 * k + 2]))
            qin_free[bi] = w.g("sS")

        def gamma_finish(mat):
            nk = 2 * gcount[0]
            gcount[0] = 0
            w.op("tensor", lambda e, nk=nk: e.matmul(
                pT[0][:, 0:nk], onesf_sb[:], gp[:, 0:nk], start=True, stop=True),
                inc=("sT", 1), wait=[w.g("sS"), gates["pt"][0]])
            w.op("vector", lambda e, mat=mat, nk=nk: e.reduce_sum(
                gtot[:, mat:mat + 1], pT[0][:, 0:nk], axis=AX), wait=[w.g("sT")])
            w.op("vector", lambda e, mat=mat: e.tensor_scalar(
                gtot[:, mat:mat + 1], gtot[:, mat:mat + 1], 1.0 / (D * F), EPS_Q,
                ALU.mult, ALU.add))
            w.op("vector", lambda e, mat=mat: e.reciprocal(
                invg[:, mat:mat + 1], gtot[:, mat:mat + 1]))
            gates["pt"][0] = w.g("sV")

        def quant_passes(mat, bi, si, qv, target, tgate):
            ig = invg[:, mat:mat + 1]
            sa, sb = si % 2, (si + 1) % 2
            w.op("vector", lambda e, bi=bi, sa=sa, ig=ig: e.tensor_scalar(
                STG[sa], QBUF[bi], ig, 0.5, ALU.mult, ALU.is_ge), wait=[qv])
            w.op("vector", lambda e, bi=bi, sb=sb, ig=ig: e.tensor_scalar(
                STG[sb], QBUF[bi], ig, -0.5, ALU.mult, ALU.is_le))
            qin_free[bi] = w.g("sV")
            shp = target
            w.op("vector", lambda e, sa=sa, sb=sb, shp=shp: e.tensor_tensor(
                shp, STG[sa].rearrange("p (c s) -> p c s", s=256),
                STG[sb].rearrange("p (c s) -> p c s", s=256), ALU.subtract),
                wait=tgate)
            return w.g("sV")

        def quant_w1_slice(fs):
            sl = fs % 2
            for h in range(2):
                bi = (2 * fs + h) % NQB
                qv = q_load(w1_src(fs, h), bi, 0)
                target = w1b[:, sl, :, h * 256:(h + 1) * 256]
                tg = list(w1slot_free[sl]) + [stage1_done[max(0, fs - 1)]]
                sv = quant_passes(0, bi, (2 * fs + h) % 2, qv, target, tg)
                dst = w1q_t.ap()[fs, :, :, h * 256:(h + 1) * 256]
                pend_writes.append((dst, target, sv,
                                    (0, fs) if h == 1 else None))
            w1mark[fs] = w.g("sV")
            while len(pend_writes) > 2:
                flush_write()

        def quant_w2_unit(dq, gate_extra, late=False):
            while pend_writes:
                flush_write()
            sl = dq % 2
            slot = w2s[:, WA] if sl == 0 else w2s[:, 0]
            for fq in range(NFQ2):
                if late:
                    bi = 0
                else:
                    bi = [0, 2, 3][fq % 3] if NQB == 4 else 0
                qv = q_load(w2_src(dq, fq), bi, 0)
                target = slot[:, fq * NW2R:(fq + 1) * NW2R, :]
                tg = list(w2slot_free[sl]) + (gate_extra if fq == 0 else [])
                sv = quant_passes(1, bi, fq % 2, qv, target, tg)
                dst = w2q_t.ap()[dq, :, fq * NW2R:(fq + 1) * NW2R, :]
                pend_writes.append((dst, target, sv,
                                    (1, dq) if fq == NFQ2 - 1 else None))
            w2mark[dq] = w.g("sV")
            while len(pend_writes) > 2:
                flush_write()

        qin2b_last = [G0]

        def quant_pre():
            for fs in range(FS):
                for h in range(2):
                    gamma_chunk(0, w1_src(fs, h), (2 * fs + h) % NQB)
            gamma_finish(0)
            g1mm = w.g("sT")
            for dq in range(4):
                for fq in range(NFQ2):
                    gamma_chunk(1, w2_src(dq, fq), (4 * dq + fq) % NQB,
                                extra=g1mm if dq == 0 and fq == 0 else None)
            quant_w1_slice(0)
            quant_w1_slice(1)

        def quant_fs_hook(fs):
            if 1 <= fs <= FS - 2:
                quant_w1_slice(fs + 1)
            if fs == FS - 2:
                gamma_finish(1)
            if fs == FS:
                while pend_writes:
                    flush_write()
                alldS = [w.g(s) for s in SSEM]
                ga = [qin_free[1], ("dQb", w.c["dQb"]), ("sS", w.c["sS"])]
                quant_w2_unit(0, ga)
                quant_w2_unit(1, [stage1_done[FS]] + alldS)

        def quant_dq_hook(dq):
            if dq == 1:
                quant_w2_unit(2, [stage2_done[0]], late=True)
            elif dq == 2:
                quant_w2_unit(3, [stage2_done[1]], late=True)
            elif dq == 4:
                while pend_writes:
                    flush_write()
                qdone = [w.g(s) for s in SSEM] + [w.g("sV")]
                gates["w2q"] = [qdone] * 4
                gates["qext"] = [w.g("sV"), w.g("sS")]
                gates["aw2"] = qdone

        def ln_stats(tiles, v_waits, s_waits):
            ntl = len(tiles)
            t0 = tiles[0]
            for i, t in enumerate(tiles):
                st_ap = stream[:, t, :]
                w.op("vector", lambda e, st_ap=st_ap, t=t: e.reduce_sum(
                    ssum[:, t:t + 1], st_ap, axis=AX, negate=True),
                    wait=(v_waits if i == 0 else None))
                w.op("scalar", lambda e, st_ap=st_ap, t=t: e.activation(
                    sqs[:], st_ap, ACTF.Square, accum_out=ssq[:, t:t + 1]),
                    wait=(s_waits if i == 0 else None))
            sq_ss = w.g("sS")
            w.op("vector", lambda e, t0=t0, ntl=ntl: e.tensor_scalar(
                mneg[:, t0:t0 + ntl], ssum[:, t0:t0 + ntl], 1.0 / D, None, ALU.mult),
                wait=[sq_ss])
            w.op("vector", lambda e, t0=t0, ntl=ntl: e.tensor_scalar(
                ex2e[:, t0:t0 + ntl], ssq[:, t0:t0 + ntl], 1.0 / D, EPS_LN,
                ALU.mult, ALU.add))
            w.op("vector", lambda e, t0=t0, ntl=ntl: e.tensor_tensor(
                m2v[:, t0:t0 + ntl], mneg[:, t0:t0 + ntl], mneg[:, t0:t0 + ntl],
                ALU.mult))
            w.op("vector", lambda e, t0=t0, ntl=ntl: e.tensor_tensor(
                vpe[:, t0:t0 + ntl], ex2e[:, t0:t0 + ntl], m2v[:, t0:t0 + ntl],
                ALU.subtract))
            w.op("vector", lambda e, t0=t0, ntl=ntl: e.reciprocal(
                rcv[:, t0:t0 + ntl], vpe[:, t0:t0 + ntl]))
            rc_sv = w.g("sV")
            w.op("scalar", lambda e, t0=t0, ntl=ntl: e.activation(
                rinv[:, t0:t0 + ntl], rcv[:, t0:t0 + ntl], ACTF.Sqrt),
                wait=[rc_sv])
            ri_ss = w.g("sS")
            w.op("vector", lambda e, t0=t0, ntl=ntl: e.tensor_tensor(
                nmr[:, t0:t0 + ntl], mneg[:, t0:t0 + ntl], rinv[:, t0:t0 + ntl],
                ALU.mult), wait=[ri_ss])

        def ln_tr(tiles, dest_fn, qwait=None):
            marks = []
            for i, t in enumerate(tiles):
                st_ap = stream[:, t, :]
                xs = i % 2
                w.op("vector", lambda e, st_ap=st_ap, t=t, xs=xs: e.tensor_scalar(
                    xm[:, xs, :], st_ap, rinv[:, t:t + 1], nmr[:, t:t + 1],
                    ALU.mult, ALU.add), wait=[gates["xm"][xs]])
                xm_v = w.g("sV")
                for cb in range(2):
                    if i % 2 == 0:
                        bank = pT[cb]
                        glist, gi = gates["pt"], cb
                    else:
                        bank = pA[cb]
                        glist, gi = gates["pa"], cb
                    for cc in range(4):
                        ch = cb * 4 + cc
                        w.op("tensor", lambda e, xs=xs, ch=ch, cc=cc, bank=bank: e.matmul(
                            bank[:, cc * 128:(cc + 1) * 128],
                            xm[:, xs, ch * 128:(ch + 1) * 128], id_sb[:],
                            start=True, stop=True),
                            inc=("sT", 1) if cc == 3 else None,
                            wait=[xm_v, glist[gi]] if cc == 0 else None)
                    dst = dest_fn(i, t, cb)
                    cw = [w.g("sT")] + (qwait if (i == 0 and cb == 0 and qwait) else [])
                    w.op("scalar", lambda e, dst=dst, bank=bank: e.copy(dst, bank[:]),
                         inc=("sS", 1), wait=cw)
                    glist[gi] = w.g("sS")
                gates["xm"][xs] = w.g("sT")
                marks.append((w.g("sV"), w.g("sS")))
            return marks

        def load_x_group(s, g):
            src = x[s, g * GT:(g + 1) * GT, :].rearrange("(t p) d -> p t d", p=128)
            extra = list(gates["qext"]) if g >= 1 else []
            if s == 0 and g == 1 and NQB == 4 and not gates["qext"]:
                extra = [qin_free[2], qin_free[3],
                         ("dQc", w.c["dQc"]), ("dQd", w.c["dQd"])]
            w.dma("sync", stream[:, g * TPG:(g + 1) * TPG, :], src, "dX",
                  wait=[gates["store"][g % 2]] + extra)
            return w.g("dX")

        def ffn_dest(i, t, cb):
            tl = t % TPG
            return xlng[:, 0, cb * 4:cb * 4 + 4, tl * 128:(tl + 1) * 128]

        def ffn_phase(s, is_p1, pre_marks=None, fs_hook=None, dq_hook=None,
                      tr_hook=None):
            for g in range(NG):
                tiles = list(range(g * TPG, (g + 1) * TPG))
                if g == 0 and pre_marks is None and is_p1:
                    if gates.get("xpre"):
                        xw = gates["xpre"][0]
                        gates["xpre"] = []
                    else:
                        xw = load_x_group(s, 0)
                    ln_stats(tiles, [xw], [xw, ("sV", 1)])
                if g == 0 and pre_marks is not None:
                    marks = pre_marks
                else:
                    marks = ln_tr(tiles, ffn_dest)
                xln_rdy = marks[-1]

                NH = GT // 512
                startup = (fs_hook is not None and g == 0)
                for fs in range(FS):
                    sl = fs % 2
                    if startup:
                        fs_hook(fs)
                        wv = w1mark[fs]
                    else:
                        wsem = "dW1a" if sl == 0 else "dW1b"
                        w.dma("sync", w1b[:, sl], w1q_t.ap()[fs], wsem,
                              wait=[gates["w1"][sl]] + gates["w1q"][fs]
                              + gates["aw2"])
                        wv = w.g(wsem)
                    for fj in range(4):
                        fc = fs * 4 + fj
                        par = fc % 3
                        if par == 0:
                            banks, glist, g0i = (pA[0], pA[1]), gates["pa"], 0
                        elif par == 1:
                            banks, glist, g0i = (pA[2], pA[3]), gates["pa"], 2
                        else:
                            banks, glist, g0i = (pT[0], pT[1]), gates["pt"], 0
                        for ci in range(DC):
                            stat = w1b[:, sl, ci, fj * 128:(fj + 1) * 128]
                            for hf in range(NH):
                                bk = banks[hf]
                                w.op("tensor", lambda e, stat=stat, bk=bk, ci=ci, hf=hf: e.matmul(
                                    bk[:], stat, xlng[:, 0, ci, hf * 512:(hf + 1) * 512],
                                    start=(ci == 0), stop=(ci == DC - 1)),
                                    wait=[wv, xln_rdy[0], xln_rdy[1],
                                          glist[g0i], glist[g0i + 1]]
                                    if (ci == 0 and hf == 0) else None,
                                    inc=("sT", 1) if ci == DC - 1 else None)
                        stv = w.g("sT")
                        if startup:
                            use_v = False
                        else:
                            use_v = (fc % 2 == 0)
                        eng = "vector" if use_v else "scalar"
                        sem = "sV" if use_v else "sS"
                        for hf in range(NH):
                            bk = banks[hf]
                            if eng == "vector":
                                w.op(eng, lambda e, fc=fc, hf=hf, bk=bk: e.tensor_scalar(
                                    h_sb[:, 0, fc, hf * 512:(hf + 1) * 512], bk[:],
                                    b1r_sb[:, fc:fc + 1], 0.0, ALU.add, ALU.max),
                                    inc=(sem, 1), wait=[stv] if hf == 0 else None)
                            else:
                                w.op(eng, lambda e, fc=fc, hf=hf, bk=bk: e.activation(
                                    h_sb[:, 0, fc, hf * 512:(hf + 1) * 512], bk[:],
                                    ACTF.Relu, bias=b1r_sb[:, fc:fc + 1]),
                                    inc=(sem, 1), wait=[stv] if hf == 0 else None)
                        glist[g0i] = w.g(sem)
                        glist[g0i + 1] = w.g(sem)
                        gates["h"][fc % 2] = w.g(sem)
                    gates["w1"][sl] = w.g("sT")
                    if startup:
                        stage1_done[fs + 1] = ("sT", w.c["sT"])
                if startup:
                    fs_hook(FS)
                h_rdy = (gates["h"][0], gates["h"][1])

                if g + 1 < NG:
                    ntiles = list(range((g + 1) * TPG, (g + 2) * TPG))
                    if is_p1:
                        xw2 = load_x_group(s, g + 1)
                        ln_stats(ntiles, [xw2], [xw2])
                    else:
                        ln_stats(ntiles, None, None)
                if tr_hook is not None and g == NG - 1:
                    tr_hook()

                for dq in range(4):
                    buf = w2s[:, WA] if dq % 2 == 0 else w2s[:, 0]
                    if startup:
                        dq_hook(dq)
                        wv = w2mark[dq]
                    else:
                        if dq % 2 == 0:
                            bg = [gates["w2"][0]]
                        else:
                            bg = [gates["w2"][1], gates["w1"][0], gates["w1"][1]]
                        wsem = "dW2a" if dq % 2 == 0 else "dW2b"
                        w.dma("sync", buf, w2q_t.ap()[dq], wsem,
                              wait=bg + gates["w2q"][dq])
                        wv = w.g(wsem)
                    for tb in range(TPG):
                        bank = pY[tb % 2]
                        t_glob = g * TPG + tb
                        for fc in range(FC):
                            stat = h_sb[:, 0, fc, tb * 128:(tb + 1) * 128]
                            w.op("tensor", lambda e, stat=stat, bank=bank, buf=buf, fc=fc: e.matmul(
                                bank[:, 0:256], stat, buf[:, fc, :],
                                start=(fc == 0), stop=False),
                                wait=[wv, h_rdy[0], h_rdy[1], gates["py"][tb % 2]]
                                if fc == 0 else None)
                        w.op("tensor", lambda e, bank=bank, dq=dq: e.matmul(
                            bank[:, 0:256], onesb_sb[0:1, 0:128],
                            b2bf[0:1, dq * 256:(dq + 1) * 256], start=False, stop=True),
                            inc=("sT", 1))
                        w.op("vector",
                             lambda e, bank=bank, t_glob=t_glob, dq=dq: e.scalar_tensor_tensor(
                                 stream[:, t_glob, dq * 256:(dq + 1) * 256], bank[:, 0:256],
                                 0.5, stream[:, t_glob, dq * 256:(dq + 1) * 256],
                                 ALU.mult, ALU.add),
                             inc=("sV", 1), wait=[w.g("sT")])
                        gates["py"][tb % 2] = w.g("sV")
                    gates["w2"][dq % 2] = w.g("sT")
                    if startup:
                        stage2_done[dq + 1] = ("sT", w.c["sT"])
                if startup:
                    dq_hook(4)
                gates["aw"][0] = w.g("sT")
                gates["w1"] = [w.g("sT"), w.g("sT")]

                if is_p1:
                    ln_stats(tiles, [("sV", c["sV"])], [("sV", c["sV"])])
                if not is_p1:
                    dst = out[s, g * GT:(g + 1) * GT, :].rearrange("(t p) d -> p t d", p=128)
                    w.dma("sync", dst, stream[:, g * TPG:(g + 1) * TPG, :], "dO",
                          wait=[w.g("sV")])
                    gates["store"][g % 2] = w.g("dO")
                    if g == 0 and s + 1 < NSEQ:
                        gates["xpre"] = [load_x_group(s + 1, 0)]
                    if g == NG - 1 and s + 1 < NSEQ:
                        xw3 = gates["xpre"][0] if gates["xpre"] else G0
                        gates["xpre"] = []
                        ln_stats(list(range(TPG)), [xw3], [xw3])
                        gates["premarks"] = ln_tr(list(range(TPG)), ffn_dest)

        def p2_dest(i, t, cb):
            return xln2[:, 0, cb * 4:cb * 4 + 4, t * 128:(t + 1) * 128]

        p2pre = {"marks": None}

        def p2_tr_hook():
            p2pre["marks"] = ln_tr(list(range(TPG)), p2_dest,
                                   qwait=list(gates["aw2"]))

        def attn_phase(s):
            pm = p2pre["marks"] if p2pre["marks"] is not None else []
            p2pre["marks"] = None
            p2marks = pm + ln_tr(list(range(len(pm), NT)), p2_dest,
                                 qwait=list(gates["aw2"]))

            w.dma("gpsimd", wkb[:, 0], wk_t.ap().rearrange("(c p) d -> p c d", p=128),
                  "dG", wait=[gates["aw"][0]] + gates["aw2"])
            kv_dg = w.g("dG")
            w.dma("gpsimd", wkb[:, 1], wv_t.ap().rearrange("(c p) d -> p c d", p=128),
                  "dG", wait=[gates["aw"][0]] + gates["aw2"])
            wv_dg = w.g("dG")

            for co in range(DC):
                for ch in range(SC):
                    bi = (co * SC + ch) % 4
                    bank = pA[bi]
                    mk = p2marks[min(ch * 4 + 3, NT - 1)]
                    for ci in range(DC):
                        w.op("tensor", lambda e, co=co, ch=ch, ci=ci, bank=bank: e.matmul(
                            bank[:], wkb[:, 0, ci, co * 128:(co + 1) * 128],
                            xln2[:, 0, ci, ch * 512:(ch + 1) * 512],
                            start=(ci == 0), stop=(ci == DC - 1)),
                            wait=[kv_dg, mk[0], mk[1],
                                  gates["pa"][bi]] if ci == 0 else None,
                            inc=("sT", 1) if ci == DC - 1 else None)
                    w.op("vector", lambda e, co=co, ch=ch, bank=bank: e.tensor_copy(
                        K_view[:, 0, co, ch * 512:(ch + 1) * 512], bank[:]),
                        inc=("sV", 1), wait=[w.g("sT")])
                    gates["pa"][bi] = w.g("sV")

            w.dma("gpsimd", wkb[:, 0], wq_t.ap().rearrange("(c p) d -> p c d", p=128),
                  "dG", wait=[w.g("sT")])
            wq_dg = w.g("dG")

            for tb in range(NT):
                for dh in range(2):
                    bi = (tb * 2 + dh) % 4
                    bank = pA[bi]
                    mk = p2marks[tb]
                    for ci in range(DC):
                        w.op("tensor", lambda e, tb=tb, dh=dh, ci=ci, bank=bank: e.matmul(
                            bank[:], xln2[:, 0, ci, tb * 128:(tb + 1) * 128],
                            wkb[:, 1, ci, dh * 512:(dh + 1) * 512],
                            start=(ci == 0), stop=(ci == DC - 1)),
                            wait=[gates["pa"][bi], wv_dg, mk[0], mk[1]]
                            if ci == 0 else None,
                            inc=("sT", 1) if ci == DC - 1 else None)
                    w.op("scalar", lambda e, tb=tb, dh=dh, bank=bank: e.copy(
                        V_view[:, 1, tb, dh * 512:(dh + 1) * 512], bank[:]),
                        inc=("sS", 1), wait=[w.g("sT")])
                    gates["pa"][bi] = w.g("sS")

            tile_st = {}

            def emit_scores(qt):
                qc = qt // 4
                if qt % 4 == 0:
                    for co in range(DC):
                        bi = co % 4
                        bank = pA[bi]
                        for ci in range(DC):
                            w.op("tensor", lambda e, co=co, ci=ci, qc=qc, bank=bank: e.matmul(
                                bank[:], wkb[:, 0, ci, co * 128:(co + 1) * 128],
                                xln2[:, 0, ci, qc * 512:(qc + 1) * 512],
                                start=(ci == 0), stop=(ci == DC - 1)),
                                wait=[wq_dg, gates["pa"][bi]] if ci == 0 else None,
                                inc=("sT", 1) if ci == DC - 1 else None)
                        w.op("vector", lambda e, co=co, bank=bank: e.tensor_copy(
                            q_view[:, 3, co, :], bank[:]),
                            inc=("sV", 1), wait=[w.g("sT")])
                        gates["pa"][bi] = w.g("sV")
                q_sv = w.g("sV")
                L = (qt + 1) * 128
                nch = (L + 511) // 512
                order = [nch - 1] + list(range(nch - 1))
                for ch in order:
                    W_ = min(512, L - ch * 512)
                    bank = pA[ch]
                    for ci in range(DC):
                        w.op("tensor", lambda e, ch=ch, ci=ci, W_=W_, qt=qt, bank=bank: e.matmul(
                            bank[:, 0:W_],
                            q_view[:, 3, ci, (qt % 4) * 128:(qt % 4 + 1) * 128],
                            K_view[:, 0, ci, ch * 512:ch * 512 + W_],
                            start=(ci == 0), stop=(ci == DC - 1)),
                            wait=[q_sv, gates["pa"][ch]] if ci == 0 else None,
                            inc=("sT", 1) if ci == DC - 1 else None)
                    if ch == nch - 1:
                        lw = L - ch * 512
                        w.op("vector", lambda e, ch=ch, lw=lw: e.tensor_tensor(
                            pA[ch][:, lw - 128:lw], pA[ch][:, lw - 128:lw],
                            mask_sb[:], ALU.add),
                            wait=[w.g("sT"), gates["prevexp"][0]])
                    W2_ = min(512, L - ch * 512)
                    w.op("vector", lambda e, ch=ch, W2_=W2_: e.reduce_max(
                        cmax[:, ch:ch + 1], pA[ch][:, 0:W2_], axis=AX),
                        wait=[w.g("sT"), gates["prevexp"][0]] if ch != nch - 1 else None)
                w.op("vector", lambda e, nch=nch: e.reduce_max(
                    negb[:], cmax[:, 0:nch], axis=AX, negate=True))
                w.op("vector", lambda e: e.tensor_scalar(
                    negb[:], negb[:], 1.0 / 32.0, None, ALU.mult), inc=("sV", 1))
                mx_sv = w.g("sV")
                pb = pslot[:, 4, :] if qt % 2 == 0 else pbuf2
                exp_marks = []
                for ch in range(nch):
                    W_ = min(512, L - ch * 512)
                    w.op("scalar", lambda e, ch=ch, W_=W_, pb=pb: e.activation(
                        pb[ch * 512:ch * 512 + W_] if False else pb[:, ch * 512:ch * 512 + W_],
                        pA[ch][:, 0:W_], ACTF.Exp,
                        bias=negb[:, 0:1], scale=1.0 / 32.0,
                        accum_out=rs_sb[:, ch:ch + 1]),
                        inc=("sS", 1), wait=[mx_sv] if ch == 0 else None)
                    gates["pa"][ch] = w.g("sS")
                    exp_marks.append(w.g("sS"))
                gates["prevexp"][0] = w.g("sS")
                w.op("vector", lambda e, nch=nch: e.reduce_sum(
                    rsum[:], rs_sb[:, 0:nch], axis=AX), wait=[w.g("sS")])
                w.op("vector", lambda e, qt=qt: e.reciprocal(
                    rinvp[:, qt % 2:qt % 2 + 1], rsum[:]), inc=("sV", 1))
                tile_st[qt] = (exp_marks, w.g("sV"), pb)

            def emit_av(qt):
                exp_marks, ri_sv, pb = tile_st.pop(qt)
                nj = qt + 1
                for jb in range((nj + 3) // 4):
                    bank = pT[jb % 2]
                    njb = min(4, nj - jb * 4)
                    for jj in range(njb):
                        j = jb * 4 + jj
                        w.op("tensor", lambda e, j=j, jj=jj, bank=bank, pb=pb: e.matmul(
                            bank[:, jj * 128:(jj + 1) * 128],
                            pb[:, j * 128:(j + 1) * 128], id_sb[:],
                            start=True, stop=True),
                            wait=[exp_marks[jb], gates["pt"][jb % 2]] if jj == 0 else None,
                            inc=("sT", 1) if jj == njb - 1 else None)
                    w.op("scalar", lambda e, jb=jb, njb=njb, bank=bank: e.copy(
                        pslot[:, 5, jb * 512:jb * 512 + njb * 128], bank[:, 0:njb * 128]),
                        inc=("sS", 1), wait=[w.g("sT")])
                    gates["pt"][jb % 2] = w.g("sS")
                    aT_ss = w.g("sS")
                    for jj in range(njb):
                        j = jb * 4 + jj
                        for dh in range(2):
                            w.op("tensor", lambda e, j=j, dh=dh, qt=qt: e.matmul(
                                pY[dh][:], pslot[:, 5, j * 128:(j + 1) * 128],
                                V_view[:, 1, j, dh * 512:(dh + 1) * 512],
                                start=(j == 0), stop=(j == qt)),
                                wait=[aT_ss, gates["py"][0], gates["py"][1]]
                                if (jj == 0 and dh == 0) else None,
                                inc=("sT", 1) if (j == qt and dh == 1) else None)
                yt = w.g("sT")
                for dh in range(2):
                    w.op("vector", lambda e, dh=dh, qt=qt: e.scalar_tensor_tensor(
                        stream[:, qt, dh * 512:(dh + 1) * 512], pY[dh][:],
                        rinvp[:, qt % 2:qt % 2 + 1],
                        stream[:, qt, dh * 512:(dh + 1) * 512],
                        ALU.mult, ALU.add),
                        inc=("sV", 1), wait=[yt, ri_sv] if dh == 0 else None)
                    gates["py"][dh] = w.g("sV")
                if qt == TPG - 1:
                    gv = w.g("sV")
                    ln_stats(list(range(TPG)), [gv], [gv])

            for qt in range(NT):
                emit_scores(qt)
                if qt > 0:
                    emit_av(qt - 1)
            emit_av(NT - 1)
            gates["p2v"][0] = w.g("sV")
            gates["aw"][0] = w.g("sT")
            gates["w1"] = [w.g("sT"), w.g("sT")]
            gates["w2"] = [w.g("sT"), w.g("sT")]

        xw0 = load_x_group(0, 0)
        ln_stats(list(range(TPG)), [xw0], [xw0, ("sV", 1)])
        pre_marks = ln_tr(list(range(TPG)), ffn_dest)
        quant_pre()
        for s in range(NSEQ):
            if s == 0:
                pm = pre_marks
            else:
                pm = gates["premarks"]
                gates["premarks"] = None
            ffn_phase(s, True, pre_marks=pm,
                      fs_hook=quant_fs_hook if s == 0 else None,
                      dq_hook=quant_dq_hook if s == 0 else None,
                      tr_hook=p2_tr_hook if NG > 1 else None)
            attn_phase(s)
            ffn_phase(s, False)

        w.op("sync", lambda e: e.wait_ge(sems["dO"], c["dO"]))
        w.op("gpsimd", lambda e: e.wait_ge(sems["dG"], c["dG"]))

    with nc.Block() as block:
        for ename, dec in (("sync", block.sync), ("gpsimd", block.gpsimd),
                           ("vector", block.vector), ("scalar", block.scalar),
                           ("tensor", block.tensor)):
            def mk(en):
                def f(eng):
                    walk(Walk(en, eng, sems))
                return f
            dec(mk(ename))
    ctx.close()
    return nc


def make_in_maps(L0_attn_x, L0_attn_w_q, L0_attn_w_k, L0_attn_w_v,
                 L0_ffn_W1, L0_ffn_b1, L0_ffn_W2, L0_ffn_b2,
                 NSEQ=2, F=4096, n_cores=N_CORES):
    FC = F // 128
    id128 = np.eye(128, dtype=ml_dtypes.bfloat16)
    onesf = np.ones((128, 128), np.float32)
    onesb = np.ones((1, 512), ml_dtypes.bfloat16)
    mask = np.where(np.arange(128)[None, :] <= np.arange(128)[:, None],
                    np.float32(0.0), np.float32(-1e30)).astype(np.float32)
    w1t = np.ascontiguousarray(np.asarray(L0_ffn_W1, np.float32).T)
    w2t = np.ascontiguousarray(np.asarray(L0_ffn_W2, np.float32).T)
    b1r = np.ascontiguousarray(np.asarray(L0_ffn_b1, np.float32).reshape(FC, 128).T)
    b2f = np.ascontiguousarray(np.asarray(L0_ffn_b2, np.float32).reshape(1, D))
    xfull = np.asarray(L0_attn_x, np.float32)
    in_maps = []
    for core in range(n_cores):
        xs = np.ascontiguousarray(xfull[core * NSEQ:(core + 1) * NSEQ])
        in_maps.append({
            "x": xs, "w1t": w1t, "w2t": w2t,
            "wq": np.ascontiguousarray(np.asarray(L0_attn_w_q, np.float32)),
            "wk": np.ascontiguousarray(np.asarray(L0_attn_w_k, np.float32)),
            "wv": np.ascontiguousarray(np.asarray(L0_attn_w_v, np.float32)),
            "b1r": b1r, "b2f": b2f,
            "id128": id128, "onesf": onesf, "onesb": onesb, "mask": mask,
        })
    return in_maps


def kernel(L0_attn_x, L0_attn_w_q, L0_attn_w_k, L0_attn_w_v,
           L0_ffn_W1, L0_ffn_b1, L0_ffn_W2, L0_ffn_b2):
    T, NSEQ, F = 2048, 2, 4096
    nc = build_program(T=T, NSEQ=NSEQ, F=F)
    in_maps = make_in_maps(L0_attn_x, L0_attn_w_q, L0_attn_w_k, L0_attn_w_v,
                           L0_ffn_W1, L0_ffn_b1, L0_ffn_W2, L0_ffn_b2,
                           NSEQ=NSEQ, F=F)
    res = run_bass_kernel_spmd(nc, in_maps, core_ids=list(range(N_CORES)))
    outs = [np.asarray(r["out"]) for r in res.results]
    return np.concatenate(outs, axis=0).reshape(16, 2048, 1024)


# version 23
# speedup vs baseline: 1.0126x; 1.0010x over previous
import numpy as np
import ml_dtypes
from contextlib import ExitStack
from collections import Counter

import concourse.bass as bass
import concourse.mybir as mybir
from concourse.bass_utils import run_bass_kernel_spmd

F32 = mybir.dt.float32
BF16 = mybir.dt.bfloat16
ALU = mybir.AluOpType
ACTF = mybir.ActivationFunctionType
AX = mybir.AxisListType.X

D = 1024
DC = D // 128
EPS_LN = 1e-5
EPS_Q = 1e-5
N_CORES = 8
G0 = ("sV", 0)


class Walk:
    def __init__(self, ename, eng, sems):
        self.ename = ename
        self.eng = eng
        self.sems = sems
        self.c = Counter()

    AUTO = {"vector": "sV", "scalar": "sS"}

    def op(self, target, fn, inc=None, wait=None, is_dma=False):
        auto = None if is_dma else self.AUTO.get(target)
        if auto is not None:
            assert inc is None or inc == (auto, 1), f"bad inc {inc} on {target}"
        if wait:
            for sem, val in wait:
                assert val <= self.c[sem], \
                    f"wait {sem}>={val} ahead of counter {self.c[sem]} ({self.ename})"
                if val > 0 and self.ename == target:
                    self.eng.wait_ge(self.sems[sem], val)
        if self.ename == target:
            inst = fn(self.eng)
            if auto is not None:
                if self.c[auto] > 0:
                    inst._wait_ge(self.sems[auto], self.c[auto])
                inst.then_inc(self.sems[auto], 1)
            elif inc is not None:
                inst.then_inc(self.sems[inc[0]], inc[1])
        if auto is not None:
            self.c[auto] += 1
        elif inc is not None:
            self.c[inc[0]] += inc[1]

    def dma(self, target, out, in_, sem, wait=None):
        self_wait = [(sem, self.c[sem])]
        self.op(target, lambda e: e.dma_start(out=out, in_=in_), inc=(sem, 16),
                wait=self_wait + (wait or []), is_dma=True)

    def g(self, sem):
        return (sem, self.c[sem])


def build_program(T=2048, NSEQ=2, F=4096):
    assert T % 512 == 0 and F % 1024 == 0
    FC = F // 128
    FS = F // 512
    NT = T // 128
    GT = min(1024, T)
    TPG = GT // 128
    NG = T // GT
    SC = T // 512

    nc = bass.Bass("TRN2", target_bir_lowering=False, debug=False)

    x_t = nc.dram_tensor("x", [NSEQ, T, D], F32, kind="ExternalInput")
    w1t_t = nc.dram_tensor("w1t", [D, F], F32, kind="ExternalInput")
    w2t_t = nc.dram_tensor("w2t", [F, D], F32, kind="ExternalInput")
    wq_t = nc.dram_tensor("wq", [D, D], F32, kind="ExternalInput")
    wk_t = nc.dram_tensor("wk", [D, D], F32, kind="ExternalInput")
    wv_t = nc.dram_tensor("wv", [D, D], F32, kind="ExternalInput")
    b1r_t = nc.dram_tensor("b1r", [128, FC], F32, kind="ExternalInput")
    b2f_t = nc.dram_tensor("b2f", [1, D], F32, kind="ExternalInput")
    id_t = nc.dram_tensor("id128", [128, 128], BF16, kind="ExternalInput")
    onesf_t = nc.dram_tensor("onesf", [128, 128], F32, kind="ExternalInput")
    onesb_t = nc.dram_tensor("onesb", [1, 512], BF16, kind="ExternalInput")
    mask_t = nc.dram_tensor("mask", [128, 128], F32, kind="ExternalInput")
    out_t = nc.dram_tensor("out", [NSEQ, T, D], F32, kind="ExternalOutput")
    w1q_t = nc.dram_tensor("w1q", [FS, 128, DC, 512], BF16, kind="Internal")
    w2q_t = nc.dram_tensor("w2q", [4, 128, FC, 256], BF16, kind="Internal")

    x = x_t.ap()
    out = out_t.ap()

    ctx = ExitStack()
    SB = lambda n, sh, dt: ctx.enter_context(nc.sbuf_tensor(n, sh, dt))

    stream = SB("stream", [128, NT, D], F32)
    abig = SB("abig", [128, 16 * 1024], F32)
    abig_bf = abig.bitcast(BF16)
    assert FC * GT <= 32768 and 2 * DC * T <= 32768
    h_sb = abig_bf.reshape([128, 32768 // (FC * GT), FC, GT])
    K_view = abig_bf.reshape([128, 32768 // (DC * T), DC, T])
    V_view = abig_bf.reshape([128, 32768 // (NT * D), NT, D])

    amid = SB("amid", [128, 8 * 1024], F32)
    amid_bf = amid.bitcast(BF16)
    assert DC * T <= 16384 and DC * GT <= 8192
    xlng = amid_bf.reshape([128, 16384 // (DC * GT), DC, GT])
    xln2 = amid_bf.reshape([128, 16384 // (DC * T), DC, T])
    qin = amid.reshape([128, 8, 1024])
    qstg = amid_bf.reshape([128, 16, 1024])
    amid_i32 = amid.bitcast(mybir.dt.int32).reshape([128, 8, 1024])
    qin2a = amid.reshape([128, 4, 2048])
    stg2a = amid_bf.reshape([128, 8, 2048])
    stg2b = amid_bf.reshape([128, 8, 2048])

    aw = SB("aw", [128, 8 * 1024], F32)
    aw_bf = aw.bitcast(BF16)
    w1b = aw_bf.reshape([128, 16384 // (DC * 512), DC, 512])
    WA = 8192 // (FC * 256)
    w2s = aw_bf.reshape([128, 16384 // (FC * 256), FC, 256])
    wkb = aw_bf.reshape([128, 2, DC, D])
    pslot = aw_bf.reshape([128, 8, 2048])
    q_view = aw_bf.reshape([128, 4, DC, 512])
    qin2b = aw.reshape([128, 4, 2048])
    i32t2 = aw.bitcast(mybir.dt.int32).reshape([128, 4, 2048])

    xm = SB("xm", [128, 2, D], BF16)
    sqs = SB("sqs", [128, D], BF16)
    ssum = SB("ssum", [128, NT], F32)
    ssq = SB("ssq", [128, NT], F32)
    mneg = SB("mneg", [128, NT], F32)
    ex2e = SB("ex2e", [128, NT], F32)
    m2v = SB("m2v", [128, NT], F32)
    vpe = SB("vpe", [128, NT], F32)
    rcv = SB("rcv", [128, NT], F32)
    rinv = SB("rinv", [128, NT], F32)
    nmr = SB("nmr", [128, NT], F32)
    gp = SB("gp", [128, 32], F32)
    gtot = SB("gtot", [128, 2], F32)
    invg = SB("invg", [128, 2], F32)
    id_sb = SB("id_sb", [128, 128], BF16)
    onesf_sb = SB("onesf_sb", [128, 128], F32)
    onesb_sb = SB("onesb_sb", [1, 512], BF16)
    mask_sb = SB("mask_sb", [128, 128], F32)
    b1r_sb = SB("b1r_sb", [128, FC], F32)
    b2bf = SB("b2bf", [1, D], BF16)
    cmax = SB("cmax", [128, 4], F32)
    negb = SB("negb", [128, 1], F32)
    rs_sb = SB("rs_sb", [128, 4], F32)
    rsum = SB("rsum", [128, 1], F32)
    rinvp = SB("rinvp", [128, 2], F32)
    pbuf2 = SB("pbuf2", [128, 2048], BF16)

    PS = lambda n: ctx.enter_context(nc.psum_tensor(n, [128, 512], F32))
    pA = [PS("pA0"), PS("pA1"), PS("pA2"), PS("pA3")]
    pY = [PS("pY0"), PS("pY1")]
    pT = [PS("pT0"), PS("pT1")]

    sems = {n: ctx.enter_context(nc.semaphore(n))
            for n in ("dG", "sT", "sV", "sS", "dC", "dQa", "dQb", "dQc", "dQd",
                      "dQe", "dQf", "dQg", "dQh", "dSa", "dSb", "dSc", "dSd",
                      "dX", "dW1a", "dW1b", "dW2a", "dW2b", "dO")}

    def walk(w: Walk):
        c = w.c
        gates = {
            "xm": [G0, G0],
            "pt": [G0, G0],
            "pa": [G0, G0, G0, G0],
            "py": [G0, G0],
            "w1": [G0, G0],
            "w2": [G0, G0],
            "h": [G0, G0],
            "aw": [G0],
            "store": [G0, G0],
            "p2v": [G0],
            "w1q": [[G0]] * FS,
            "w2q": [[G0]] * 4,
            "prevexp": [G0],
            "qext": [],
            "xpre": [],
            "aw2": [],
            "prew1": {},
            "premarks": None,
        }

        for dst, src in ((id_sb[:], id_t.ap()), (onesf_sb[:], onesf_t.ap()),
                         (onesb_sb[:], onesb_t.ap()), (mask_sb[:], mask_t.ap()),
                         (b1r_sb[:], b1r_t.ap()), (qin2a[0:1, 2, 0:D], b2f_t.ap())):
            w.dma("sync", dst, src, "dC")
        w.op("vector", lambda e: e.tensor_copy(b2bf[:], qin2a[0:1, 2, 0:D]),
             inc=("sV", 1), wait=[w.g("dC")])

        QSEM = ["dQa", "dQb", "dQc", "dQd"]
        QBUF = [qin2a[:, 2, :], qin2b[:, 2, :]]
        NQB = 2
        if NT >= TPG + 4:
            NQB = 4
            QBUF.append(stream[:, TPG:TPG + 2, :].rearrange("p t d -> p (t d)"))
            QBUF.append(stream[:, TPG + 2:TPG + 4, :].rearrange("p t d -> p (t d)"))
        STG = [stg2a[:, 6, :], stg2b[:, 7, :]]
        qin_free = [("sV", 1), G0, G0, G0]
        SSEM = ["dSa", "dSb", "dSc", "dSd"]
        pend_writes = []
        NW2R = 8
        NFQ2 = F // 1024
        w1mark = [G0] * FS
        w2mark = [G0] * 4
        w1slot_free = [[G0], [G0]]
        w2slot_free = [[G0], [G0]]
        stage1_done = [G0] * (FS + 1)
        stage2_done = [G0] * 5
        ssem_i = [0]

        def flush_write():
            if pend_writes:
                dst, ssrc, sv, tag = pend_writes.pop(0)
                si = ssem_i[0] % 4
                ssem_i[0] += 1
                w.dma("sync", dst, ssrc, SSEM[si], wait=[sv])
                if tag is not None:
                    done = [w.g(s) for s in SSEM]
                    kind, idx = tag
                    if kind == 0:
                        gates["w1q"][idx] = done
                        w1slot_free[idx % 2] = done
                    else:
                        gates["w2q"][idx] = done
                        w2slot_free[idx % 2] = done

        def q_load(src_ap, bi, k, extra=None):
            w.dma("sync", QBUF[bi].rearrange("p (c s) -> p c s", s=256),
                  src_ap, QSEM[bi], wait=[qin_free[bi]] + (extra or []))
            return w.g(QSEM[bi])

        def w1_src(fs, h):
            return w1t_t.ap()[:, fs * 512 + h * 256:fs * 512 + (h + 1) * 256]\
                .rearrange("(c p) s -> p c s", p=128)

        def w2_src(dq, fq):
            return w2t_t.ap()[fq * 1024:(fq + 1) * 1024,
                              dq * 256:(dq + 1) * 256]\
                .rearrange("(c p) s -> p c s", p=128)

        gcount = [0]

        def gamma_chunk(mat, src_ap, bi, extra=None):
            qv = q_load(src_ap, bi, 0)
            k = gcount[0]
            gcount[0] += 1
            ws = [qv] + ([extra] if extra is not None else [])
            w.op("scalar", lambda e, k=k, bi=bi: e.activation(
                sqs[:], QBUF[bi][:, 0:1024], ACTF.Abs,
                accum_out=gp[:, 2 * k:2 * k + 1]), wait=ws)
            w.op("scalar", lambda e, k=k, bi=bi: e.activation(
                sqs[:], QBUF[bi][:, 1024:2048], ACTF.Abs,
                accum_out=gp[:, 2 * k + 1:2 * k + 2]))
            qin_free[bi] = w.g("sS")

        def gamma_finish(mat):
            nk = 2 * gcount[0]
            gcount[0] = 0
            w.op("tensor", lambda e, nk=nk: e.matmul(
                pT[0][:, 0:nk], onesf_sb[:], gp[:, 0:nk], start=True, stop=True),
                inc=("sT", 1), wait=[w.g("sS"), gates["pt"][0]])
            w.op("vector", lambda e, mat=mat, nk=nk: e.reduce_sum(
                gtot[:, mat:mat + 1], pT[0][:, 0:nk], axis=AX), wait=[w.g("sT")])
            w.op("vector", lambda e, mat=mat: e.tensor_scalar(
                gtot[:, mat:mat + 1], gtot[:, mat:mat + 1], 1.0 / (D * F), EPS_Q,
                ALU.mult, ALU.add))
            w.op("vector", lambda e, mat=mat: e.reciprocal(
                invg[:, mat:mat + 1], gtot[:, mat:mat + 1]))
            gates["pt"][0] = w.g("sV")

        def quant_passes(mat, bi, si, qv, target, tgate):
            ig = invg[:, mat:mat + 1]
            sa, sb = si % 2, (si + 1) % 2
            w.op("vector", lambda e, bi=bi, sa=sa, ig=ig: e.tensor_scalar(
                STG[sa], QBUF[bi], ig, 0.5, ALU.mult, ALU.is_ge), wait=[qv])
            w.op("vector", lambda e, bi=bi, sb=sb, ig=ig: e.tensor_scalar(
                STG[sb], QBUF[bi], ig, -0.5, ALU.mult, ALU.is_le))
            qin_free[bi] = w.g("sV")
            shp = target
            w.op("vector", lambda e, sa=sa, sb=sb, shp=shp: e.tensor_tensor(
                shp, STG[sa].rearrange("p (c s) -> p c s", s=256),
                STG[sb].rearrange("p (c s) -> p c s", s=256), ALU.subtract),
                wait=tgate)
            return w.g("sV")

        def quant_w1_slice(fs):
            sl = fs % 2
            for h in range(2):
                bi = (2 * fs + h) % NQB
                qv = q_load(w1_src(fs, h), bi, 0)
                target = w1b[:, sl, :, h * 256:(h + 1) * 256]
                tg = list(w1slot_free[sl]) + [stage1_done[max(0, fs - 1)]]
                sv = quant_passes(0, bi, (2 * fs + h) % 2, qv, target, tg)
                dst = w1q_t.ap()[fs, :, :, h * 256:(h + 1) * 256]
                pend_writes.append((dst, target, sv,
                                    (0, fs) if h == 1 else None))
            w1mark[fs] = w.g("sV")
            while len(pend_writes) > 2:
                flush_write()

        def quant_w2_unit(dq, gate_extra, late=False):
            while pend_writes:
                flush_write()
            sl = dq % 2
            slot = w2s[:, WA] if sl == 0 else w2s[:, 0]
            for fq in range(NFQ2):
                if late:
                    bi = 0
                else:
                    bi = [0, 2, 3][fq % 3] if NQB == 4 else 0
                qv = q_load(w2_src(dq, fq), bi, 0)
                target = slot[:, fq * NW2R:(fq + 1) * NW2R, :]
                tg = list(w2slot_free[sl]) + (gate_extra if fq == 0 else [])
                sv = quant_passes(1, bi, fq % 2, qv, target, tg)
                dst = w2q_t.ap()[dq, :, fq * NW2R:(fq + 1) * NW2R, :]
                pend_writes.append((dst, target, sv,
                                    (1, dq) if fq == NFQ2 - 1 else None))
            w2mark[dq] = w.g("sV")
            while len(pend_writes) > 2:
                flush_write()

        qin2b_last = [G0]

        def quant_pre():
            for fs in range(FS):
                for h in range(2):
                    gamma_chunk(0, w1_src(fs, h), (2 * fs + h) % NQB)
            gamma_finish(0)
            g1mm = w.g("sT")
            for dq in range(4):
                for fq in range(NFQ2):
                    gamma_chunk(1, w2_src(dq, fq), (4 * dq + fq) % NQB,
                                extra=g1mm if dq == 0 and fq == 0 else None)
            quant_w1_slice(0)
            quant_w1_slice(1)

        def quant_fs_hook(fs):
            if 1 <= fs <= FS - 2:
                quant_w1_slice(fs + 1)
            if fs == FS - 2:
                gamma_finish(1)
            if fs == FS:
                while pend_writes:
                    flush_write()
                alldS = [w.g(s) for s in SSEM]
                ga = [qin_free[1], ("dQb", w.c["dQb"]), ("sS", w.c["sS"])]
                quant_w2_unit(0, ga)
                quant_w2_unit(1, [stage1_done[FS]] + alldS)

        def quant_dq_hook(dq):
            if dq == 1:
                quant_w2_unit(2, [stage2_done[0]], late=True)
            elif dq == 2:
                quant_w2_unit(3, [stage2_done[1]], late=True)
            elif dq == 4:
                while pend_writes:
                    flush_write()
                qdone = [w.g(s) for s in SSEM] + [w.g("sV")]
                gates["w2q"] = [qdone] * 4
                gates["qext"] = [w.g("sV"), w.g("sS")]
                gates["aw2"] = qdone

        def ln_stats(tiles, v_waits, s_waits):
            ntl = len(tiles)
            t0 = tiles[0]
            for i, t in enumerate(tiles):
                st_ap = stream[:, t, :]
                w.op("vector", lambda e, st_ap=st_ap, t=t: e.reduce_sum(
                    ssum[:, t:t + 1], st_ap, axis=AX, negate=True),
                    wait=(v_waits if i == 0 else None))
                w.op("scalar", lambda e, st_ap=st_ap, t=t: e.activation(
                    sqs[:], st_ap, ACTF.Square, accum_out=ssq[:, t:t + 1]),
                    wait=(s_waits if i == 0 else None))
            sq_ss = w.g("sS")
            w.op("vector", lambda e, t0=t0, ntl=ntl: e.tensor_scalar(
                mneg[:, t0:t0 + ntl], ssum[:, t0:t0 + ntl], 1.0 / D, None, ALU.mult),
                wait=[sq_ss])
            w.op("vector", lambda e, t0=t0, ntl=ntl: e.tensor_scalar(
                ex2e[:, t0:t0 + ntl], ssq[:, t0:t0 + ntl], 1.0 / D, EPS_LN,
                ALU.mult, ALU.add))
            w.op("vector", lambda e, t0=t0, ntl=ntl: e.tensor_tensor(
                m2v[:, t0:t0 + ntl], mneg[:, t0:t0 + ntl], mneg[:, t0:t0 + ntl],
                ALU.mult))
            w.op("vector", lambda e, t0=t0, ntl=ntl: e.tensor_tensor(
                vpe[:, t0:t0 + ntl], ex2e[:, t0:t0 + ntl], m2v[:, t0:t0 + ntl],
                ALU.subtract))
            w.op("vector", lambda e, t0=t0, ntl=ntl: e.reciprocal(
                rcv[:, t0:t0 + ntl], vpe[:, t0:t0 + ntl]))
            rc_sv = w.g("sV")
            w.op("scalar", lambda e, t0=t0, ntl=ntl: e.activation(
                rinv[:, t0:t0 + ntl], rcv[:, t0:t0 + ntl], ACTF.Sqrt),
                wait=[rc_sv])
            ri_ss = w.g("sS")
            w.op("vector", lambda e, t0=t0, ntl=ntl: e.tensor_tensor(
                nmr[:, t0:t0 + ntl], mneg[:, t0:t0 + ntl], rinv[:, t0:t0 + ntl],
                ALU.mult), wait=[ri_ss])

        def ln_tr(tiles, dest_fn, qwait=None):
            marks = []
            for i, t in enumerate(tiles):
                st_ap = stream[:, t, :]
                xs = i % 2
                w.op("vector", lambda e, st_ap=st_ap, t=t, xs=xs: e.tensor_scalar(
                    xm[:, xs, :], st_ap, rinv[:, t:t + 1], nmr[:, t:t + 1],
                    ALU.mult, ALU.add), wait=[gates["xm"][xs]])
                xm_v = w.g("sV")
                for cb in range(2):
                    if i % 2 == 0:
                        bank = pT[cb]
                        glist, gi = gates["pt"], cb
                    else:
                        bank = pA[cb]
                        glist, gi = gates["pa"], cb
                    for cc in range(4):
                        ch = cb * 4 + cc
                        w.op("tensor", lambda e, xs=xs, ch=ch, cc=cc, bank=bank: e.matmul(
                            bank[:, cc * 128:(cc + 1) * 128],
                            xm[:, xs, ch * 128:(ch + 1) * 128], id_sb[:],
                            start=True, stop=True),
                            inc=("sT", 1) if cc == 3 else None,
                            wait=[xm_v, glist[gi]] if cc == 0 else None)
                    dst = dest_fn(i, t, cb)
                    cw = [w.g("sT")] + (qwait if (i == 0 and cb == 0 and qwait) else [])
                    w.op("scalar", lambda e, dst=dst, bank=bank: e.copy(dst, bank[:]),
                         inc=("sS", 1), wait=cw)
                    glist[gi] = w.g("sS")
                gates["xm"][xs] = w.g("sT")
                marks.append((w.g("sV"), w.g("sS")))
            return marks

        def load_x_group(s, g):
            src = x[s, g * GT:(g + 1) * GT, :].rearrange("(t p) d -> p t d", p=128)
            extra = list(gates["qext"]) if g >= 1 else []
            if s == 0 and g == 1 and NQB == 4 and not gates["qext"]:
                extra = [qin_free[2], qin_free[3],
                         ("dQc", w.c["dQc"]), ("dQd", w.c["dQd"])]
            w.dma("sync", stream[:, g * TPG:(g + 1) * TPG, :], src, "dX",
                  wait=[gates["store"][g % 2]] + extra)
            return w.g("dX")

        def ffn_dest(i, t, cb):
            tl = t % TPG
            return xlng[:, 0, cb * 4:cb * 4 + 4, tl * 128:(tl + 1) * 128]

        def ffn_phase(s, is_p1, pre_marks=None, fs_hook=None, dq_hook=None,
                      tr_hook=None):
            for g in range(NG):
                tiles = list(range(g * TPG, (g + 1) * TPG))
                if g == 0 and pre_marks is None and is_p1:
                    if gates.get("xpre"):
                        xw = gates["xpre"][0]
                        gates["xpre"] = []
                    else:
                        xw = load_x_group(s, 0)
                    ln_stats(tiles, [xw], [xw, ("sV", 1)])
                if g == 0 and pre_marks is not None:
                    marks = pre_marks
                else:
                    marks = ln_tr(tiles, ffn_dest)
                xln_rdy = marks[-1]

                NH = GT // 512
                startup = (fs_hook is not None and g == 0)
                for fs in range(FS):
                    sl = fs % 2
                    if startup:
                        fs_hook(fs)
                        wv = w1mark[fs]
                    else:
                        if g == 0 and fs in gates["prew1"]:
                            wv = gates["prew1"].pop(fs)
                        else:
                            wsem = "dW1a" if sl == 0 else "dW1b"
                            w.dma("sync", w1b[:, sl], w1q_t.ap()[fs], wsem,
                                  wait=[gates["w1"][sl]] + gates["w1q"][fs]
                                  + gates["aw2"])
                            wv = w.g(wsem)
                    for fj in range(4):
                        fc = fs * 4 + fj
                        par = fc % 3
                        if par == 0:
                            banks, glist, g0i = (pA[0], pA[1]), gates["pa"], 0
                        elif par == 1:
                            banks, glist, g0i = (pA[2], pA[3]), gates["pa"], 2
                        else:
                            banks, glist, g0i = (pT[0], pT[1]), gates["pt"], 0
                        for ci in range(DC):
                            stat = w1b[:, sl, ci, fj * 128:(fj + 1) * 128]
                            for hf in range(NH):
                                bk = banks[hf]
                                w.op("tensor", lambda e, stat=stat, bk=bk, ci=ci, hf=hf: e.matmul(
                                    bk[:], stat, xlng[:, 0, ci, hf * 512:(hf + 1) * 512],
                                    start=(ci == 0), stop=(ci == DC - 1)),
                                    wait=[wv, xln_rdy[0], xln_rdy[1],
                                          glist[g0i], glist[g0i + 1]]
                                    if (ci == 0 and hf == 0) else None,
                                    inc=("sT", 1) if ci == DC - 1 else None)
                        stv = w.g("sT")
                        if startup:
                            use_v = False
                        else:
                            use_v = (fc % 2 == 0)
                        eng = "vector" if use_v else "scalar"
                        sem = "sV" if use_v else "sS"
                        for hf in range(NH):
                            bk = banks[hf]
                            if eng == "vector":
                                w.op(eng, lambda e, fc=fc, hf=hf, bk=bk: e.tensor_scalar(
                                    h_sb[:, 0, fc, hf * 512:(hf + 1) * 512], bk[:],
                                    b1r_sb[:, fc:fc + 1], 0.0, ALU.add, ALU.max),
                                    inc=(sem, 1), wait=[stv] if hf == 0 else None)
                            else:
                                w.op(eng, lambda e, fc=fc, hf=hf, bk=bk: e.activation(
                                    h_sb[:, 0, fc, hf * 512:(hf + 1) * 512], bk[:],
                                    ACTF.Relu, bias=b1r_sb[:, fc:fc + 1]),
                                    inc=(sem, 1), wait=[stv] if hf == 0 else None)
                        glist[g0i] = w.g(sem)
                        glist[g0i + 1] = w.g(sem)
                        gates["h"][fc % 2] = w.g(sem)
                    gates["w1"][sl] = w.g("sT")
                    if startup:
                        stage1_done[fs + 1] = ("sT", w.c["sT"])
                if startup:
                    fs_hook(FS)
                h_rdy = (gates["h"][0], gates["h"][1])

                if g + 1 < NG:
                    ntiles = list(range((g + 1) * TPG, (g + 2) * TPG))
                    if is_p1:
                        xw2 = load_x_group(s, g + 1)
                        ln_stats(ntiles, [xw2], [xw2])
                    else:
                        ln_stats(ntiles, None, None)
                if tr_hook is not None and g == NG - 1:
                    tr_hook()

                for dq in range(4):
                    buf = w2s[:, WA] if dq % 2 == 0 else w2s[:, 0]
                    if startup:
                        dq_hook(dq)
                        wv = w2mark[dq]
                    else:
                        if dq % 2 == 0:
                            bg = [gates["w2"][0]]
                        else:
                            bg = [gates["w2"][1], gates["w1"][0], gates["w1"][1]]
                        wsem = "dW2a" if dq % 2 == 0 else "dW2b"
                        w.dma("sync", buf, w2q_t.ap()[dq], wsem,
                              wait=bg + gates["w2q"][dq])
                        wv = w.g(wsem)
                    for tb in range(TPG):
                        bank = pY[tb % 2]
                        t_glob = g * TPG + tb
                        for fc in range(FC):
                            stat = h_sb[:, 0, fc, tb * 128:(tb + 1) * 128]
                            w.op("tensor", lambda e, stat=stat, bank=bank, buf=buf, fc=fc: e.matmul(
                                bank[:, 0:256], stat, buf[:, fc, :],
                                start=(fc == 0), stop=False),
                                wait=[wv, h_rdy[0], h_rdy[1], gates["py"][tb % 2]]
                                if fc == 0 else None)
                        w.op("tensor", lambda e, bank=bank, dq=dq: e.matmul(
                            bank[:, 0:256], onesb_sb[0:1, 0:128],
                            b2bf[0:1, dq * 256:(dq + 1) * 256], start=False, stop=True),
                            inc=("sT", 1))
                        w.op("vector",
                             lambda e, bank=bank, t_glob=t_glob, dq=dq: e.scalar_tensor_tensor(
                                 stream[:, t_glob, dq * 256:(dq + 1) * 256], bank[:, 0:256],
                                 0.5, stream[:, t_glob, dq * 256:(dq + 1) * 256],
                                 ALU.mult, ALU.add),
                             inc=("sV", 1), wait=[w.g("sT")])
                        gates["py"][tb % 2] = w.g("sV")
                    gates["w2"][dq % 2] = w.g("sT")
                    if startup:
                        stage2_done[dq + 1] = ("sT", w.c["sT"])
                if startup:
                    dq_hook(4)
                gates["aw"][0] = w.g("sT")
                gates["w1"] = [w.g("sT"), w.g("sT")]

                if is_p1:
                    ln_stats(tiles, [("sV", c["sV"])], [("sV", c["sV"])])
                if not is_p1:
                    dst = out[s, g * GT:(g + 1) * GT, :].rearrange("(t p) d -> p t d", p=128)
                    w.dma("sync", dst, stream[:, g * TPG:(g + 1) * TPG, :], "dO",
                          wait=[w.g("sV")])
                    gates["store"][g % 2] = w.g("dO")
                    if g == 0 and s + 1 < NSEQ:
                        gates["xpre"] = [load_x_group(s + 1, 0)]
                    if g == NG - 1 and s + 1 < NSEQ:
                        xw3 = gates["xpre"][0] if gates["xpre"] else G0
                        gates["xpre"] = []
                        ln_stats(list(range(TPG)), [xw3], [xw3])
                        gates["premarks"] = ln_tr(list(range(TPG)), ffn_dest)

        def p2_dest(i, t, cb):
            return xln2[:, 0, cb * 4:cb * 4 + 4, t * 128:(t + 1) * 128]

        p2pre = {"marks": None}

        def p2_tr_hook():
            p2pre["marks"] = ln_tr(list(range(TPG)), p2_dest,
                                   qwait=list(gates["aw2"]))

        def attn_phase(s):
            pm = p2pre["marks"] if p2pre["marks"] is not None else []
            p2pre["marks"] = None
            p2marks = pm + ln_tr(list(range(len(pm), NT)), p2_dest,
                                 qwait=list(gates["aw2"]))

            w.dma("gpsimd", wkb[:, 0], wk_t.ap().rearrange("(c p) d -> p c d", p=128),
                  "dG", wait=[gates["aw"][0]] + gates["aw2"])
            kv_dg = w.g("dG")
            w.dma("gpsimd", wkb[:, 1], wv_t.ap().rearrange("(c p) d -> p c d", p=128),
                  "dG", wait=[gates["aw"][0]] + gates["aw2"])
            wv_dg = w.g("dG")

            for co in range(DC):
                for ch in range(SC):
                    bi = (co * SC + ch) % 4
                    bank = pA[bi]
                    mk = p2marks[min(ch * 4 + 3, NT - 1)]
                    for ci in range(DC):
                        w.op("tensor", lambda e, co=co, ch=ch, ci=ci, bank=bank: e.matmul(
                            bank[:], wkb[:, 0, ci, co * 128:(co + 1) * 128],
                            xln2[:, 0, ci, ch * 512:(ch + 1) * 512],
                            start=(ci == 0), stop=(ci == DC - 1)),
                            wait=[kv_dg, mk[0], mk[1],
                                  gates["pa"][bi]] if ci == 0 else None,
                            inc=("sT", 1) if ci == DC - 1 else None)
                    w.op("vector", lambda e, co=co, ch=ch, bank=bank: e.tensor_copy(
                        K_view[:, 0, co, ch * 512:(ch + 1) * 512], bank[:]),
                        inc=("sV", 1), wait=[w.g("sT")])
                    gates["pa"][bi] = w.g("sV")

            w.dma("gpsimd", wkb[:, 0], wq_t.ap().rearrange("(c p) d -> p c d", p=128),
                  "dG", wait=[w.g("sT")])
            wq_dg = w.g("dG")

            for tb in range(NT):
                for dh in range(2):
                    bi = (tb * 2 + dh) % 4
                    bank = pA[bi]
                    mk = p2marks[tb]
                    for ci in range(DC):
                        w.op("tensor", lambda e, tb=tb, dh=dh, ci=ci, bank=bank: e.matmul(
                            bank[:], xln2[:, 0, ci, tb * 128:(tb + 1) * 128],
                            wkb[:, 1, ci, dh * 512:(dh + 1) * 512],
                            start=(ci == 0), stop=(ci == DC - 1)),
                            wait=[gates["pa"][bi], wv_dg, mk[0], mk[1]]
                            if ci == 0 else None,
                            inc=("sT", 1) if ci == DC - 1 else None)
                    w.op("scalar", lambda e, tb=tb, dh=dh, bank=bank: e.copy(
                        V_view[:, 1, tb, dh * 512:(dh + 1) * 512], bank[:]),
                        inc=("sS", 1), wait=[w.g("sT")])
                    gates["pa"][bi] = w.g("sS")

            tile_st = {}

            def emit_scores(qt):
                qc = qt // 4
                if qt % 4 == 0:
                    for co in range(DC):
                        bi = co % 4
                        bank = pA[bi]
                        for ci in range(DC):
                            w.op("tensor", lambda e, co=co, ci=ci, qc=qc, bank=bank: e.matmul(
                                bank[:], wkb[:, 0, ci, co * 128:(co + 1) * 128],
                                xln2[:, 0, ci, qc * 512:(qc + 1) * 512],
                                start=(ci == 0), stop=(ci == DC - 1)),
                                wait=[wq_dg, gates["pa"][bi]] if ci == 0 else None,
                                inc=("sT", 1) if ci == DC - 1 else None)
                        w.op("vector", lambda e, co=co, bank=bank: e.tensor_copy(
                            q_view[:, 3, co, :], bank[:]),
                            inc=("sV", 1), wait=[w.g("sT")])
                        gates["pa"][bi] = w.g("sV")
                    if qt == NT - 4:
                        lastq = w.g("sT")
                        for pfs in range(2):
                            wsem = "dW1a" if pfs == 0 else "dW1b"
                            w.dma("sync", w1b[:, pfs], w1q_t.ap()[pfs], wsem,
                                  wait=[lastq, gates["w1"][pfs]]
                                  + gates["w1q"][pfs] + gates["aw2"])
                            gates["prew1"][pfs] = w.g(wsem)
                q_sv = w.g("sV")
                L = (qt + 1) * 128
                nch = (L + 511) // 512
                order = [nch - 1] + list(range(nch - 1))
                for ch in order:
                    W_ = min(512, L - ch * 512)
                    bank = pA[ch]
                    for ci in range(DC):
                        w.op("tensor", lambda e, ch=ch, ci=ci, W_=W_, qt=qt, bank=bank: e.matmul(
                            bank[:, 0:W_],
                            q_view[:, 3, ci, (qt % 4) * 128:(qt % 4 + 1) * 128],
                            K_view[:, 0, ci, ch * 512:ch * 512 + W_],
                            start=(ci == 0), stop=(ci == DC - 1)),
                            wait=[q_sv, gates["pa"][ch]] if ci == 0 else None,
                            inc=("sT", 1) if ci == DC - 1 else None)
                    if ch == nch - 1:
                        lw = L - ch * 512
                        w.op("vector", lambda e, ch=ch, lw=lw: e.tensor_tensor(
                            pA[ch][:, lw - 128:lw], pA[ch][:, lw - 128:lw],
                            mask_sb[:], ALU.add),
                            wait=[w.g("sT"), gates["prevexp"][0]])
                    W2_ = min(512, L - ch * 512)
                    w.op("vector", lambda e, ch=ch, W2_=W2_: e.reduce_max(
                        cmax[:, ch:ch + 1], pA[ch][:, 0:W2_], axis=AX),
                        wait=[w.g("sT"), gates["prevexp"][0]] if ch != nch - 1 else None)
                w.op("vector", lambda e, nch=nch: e.reduce_max(
                    negb[:], cmax[:, 0:nch], axis=AX, negate=True))
                w.op("vector", lambda e: e.tensor_scalar(
                    negb[:], negb[:], 1.0 / 32.0, None, ALU.mult), inc=("sV", 1))
                mx_sv = w.g("sV")
                pb = pslot[:, 4, :] if qt % 2 == 0 else pbuf2
                exp_marks = []
                for ch in range(nch):
                    W_ = min(512, L - ch * 512)
                    w.op("scalar", lambda e, ch=ch, W_=W_, pb=pb: e.activation(
                        pb[ch * 512:ch * 512 + W_] if False else pb[:, ch * 512:ch * 512 + W_],
                        pA[ch][:, 0:W_], ACTF.Exp,
                        bias=negb[:, 0:1], scale=1.0 / 32.0,
                        accum_out=rs_sb[:, ch:ch + 1]),
                        inc=("sS", 1), wait=[mx_sv] if ch == 0 else None)
                    gates["pa"][ch] = w.g("sS")
                    exp_marks.append(w.g("sS"))
                gates["prevexp"][0] = w.g("sS")
                w.op("vector", lambda e, nch=nch: e.reduce_sum(
                    rsum[:], rs_sb[:, 0:nch], axis=AX), wait=[w.g("sS")])
                w.op("vector", lambda e, qt=qt: e.reciprocal(
                    rinvp[:, qt % 2:qt % 2 + 1], rsum[:]), inc=("sV", 1))
                tile_st[qt] = (exp_marks, w.g("sV"), pb)

            def emit_av(qt):
                exp_marks, ri_sv, pb = tile_st.pop(qt)
                nj = qt + 1
                for jb in range((nj + 3) // 4):
                    bank = pT[jb % 2]
                    njb = min(4, nj - jb * 4)
                    for jj in range(njb):
                        j = jb * 4 + jj
                        w.op("tensor", lambda e, j=j, jj=jj, bank=bank, pb=pb: e.matmul(
                            bank[:, jj * 128:(jj + 1) * 128],
                            pb[:, j * 128:(j + 1) * 128], id_sb[:],
                            start=True, stop=True),
                            wait=[exp_marks[jb], gates["pt"][jb % 2]] if jj == 0 else None,
                            inc=("sT", 1) if jj == njb - 1 else None)
                    w.op("scalar", lambda e, jb=jb, njb=njb, bank=bank: e.copy(
                        pslot[:, 5, jb * 512:jb * 512 + njb * 128], bank[:, 0:njb * 128]),
                        inc=("sS", 1), wait=[w.g("sT")])
                    gates["pt"][jb % 2] = w.g("sS")
                    aT_ss = w.g("sS")
                    for jj in range(njb):
                        j = jb * 4 + jj
                        for dh in range(2):
                            w.op("tensor", lambda e, j=j, dh=dh, qt=qt: e.matmul(
                                pY[dh][:], pslot[:, 5, j * 128:(j + 1) * 128],
                                V_view[:, 1, j, dh * 512:(dh + 1) * 512],
                                start=(j == 0), stop=(j == qt)),
                                wait=[aT_ss, gates["py"][0], gates["py"][1]]
                                if (jj == 0 and dh == 0) else None,
                                inc=("sT", 1) if (j == qt and dh == 1) else None)
                yt = w.g("sT")
                for dh in range(2):
                    w.op("vector", lambda e, dh=dh, qt=qt: e.scalar_tensor_tensor(
                        stream[:, qt, dh * 512:(dh + 1) * 512], pY[dh][:],
                        rinvp[:, qt % 2:qt % 2 + 1],
                        stream[:, qt, dh * 512:(dh + 1) * 512],
                        ALU.mult, ALU.add),
                        inc=("sV", 1), wait=[yt, ri_sv] if dh == 0 else None)
                    gates["py"][dh] = w.g("sV")
                if qt == TPG - 1:
                    gv = w.g("sV")
                    ln_stats(list(range(TPG)), [gv], [gv])

            for qt in range(NT):
                emit_scores(qt)
                if qt > 0:
                    emit_av(qt - 1)
            emit_av(NT - 1)
            gates["p2v"][0] = w.g("sV")
            gates["aw"][0] = w.g("sT")
            gates["w1"] = [w.g("sT"), w.g("sT")]
            gates["w2"] = [w.g("sT"), w.g("sT")]

        xw0 = load_x_group(0, 0)
        ln_stats(list(range(TPG)), [xw0], [xw0, ("sV", 1)])
        pre_marks = ln_tr(list(range(TPG)), ffn_dest)
        quant_pre()
        for s in range(NSEQ):
            if s == 0:
                pm = pre_marks
            else:
                pm = gates["premarks"]
                gates["premarks"] = None
            ffn_phase(s, True, pre_marks=pm,
                      fs_hook=quant_fs_hook if s == 0 else None,
                      dq_hook=quant_dq_hook if s == 0 else None,
                      tr_hook=p2_tr_hook if NG > 1 else None)
            attn_phase(s)
            ffn_phase(s, False)

        w.op("sync", lambda e: e.wait_ge(sems["dO"], c["dO"]))
        w.op("gpsimd", lambda e: e.wait_ge(sems["dG"], c["dG"]))

    with nc.Block() as block:
        for ename, dec in (("sync", block.sync), ("gpsimd", block.gpsimd),
                           ("vector", block.vector), ("scalar", block.scalar),
                           ("tensor", block.tensor)):
            def mk(en):
                def f(eng):
                    walk(Walk(en, eng, sems))
                return f
            dec(mk(ename))
    ctx.close()
    return nc


def make_in_maps(L0_attn_x, L0_attn_w_q, L0_attn_w_k, L0_attn_w_v,
                 L0_ffn_W1, L0_ffn_b1, L0_ffn_W2, L0_ffn_b2,
                 NSEQ=2, F=4096, n_cores=N_CORES):
    FC = F // 128
    id128 = np.eye(128, dtype=ml_dtypes.bfloat16)
    onesf = np.ones((128, 128), np.float32)
    onesb = np.ones((1, 512), ml_dtypes.bfloat16)
    mask = np.where(np.arange(128)[None, :] <= np.arange(128)[:, None],
                    np.float32(0.0), np.float32(-1e30)).astype(np.float32)
    w1t = np.ascontiguousarray(np.asarray(L0_ffn_W1, np.float32).T)
    w2t = np.ascontiguousarray(np.asarray(L0_ffn_W2, np.float32).T)
    b1r = np.ascontiguousarray(np.asarray(L0_ffn_b1, np.float32).reshape(FC, 128).T)
    b2f = np.ascontiguousarray(np.asarray(L0_ffn_b2, np.float32).reshape(1, D))
    xfull = np.asarray(L0_attn_x, np.float32)
    in_maps = []
    for core in range(n_cores):
        xs = np.ascontiguousarray(xfull[core * NSEQ:(core + 1) * NSEQ])
        in_maps.append({
            "x": xs, "w1t": w1t, "w2t": w2t,
            "wq": np.ascontiguousarray(np.asarray(L0_attn_w_q, np.float32)),
            "wk": np.ascontiguousarray(np.asarray(L0_attn_w_k, np.float32)),
            "wv": np.ascontiguousarray(np.asarray(L0_attn_w_v, np.float32)),
            "b1r": b1r, "b2f": b2f,
            "id128": id128, "onesf": onesf, "onesb": onesb, "mask": mask,
        })
    return in_maps


def kernel(L0_attn_x, L0_attn_w_q, L0_attn_w_k, L0_attn_w_v,
           L0_ffn_W1, L0_ffn_b1, L0_ffn_W2, L0_ffn_b2):
    T, NSEQ, F = 2048, 2, 4096
    nc = build_program(T=T, NSEQ=NSEQ, F=F)
    in_maps = make_in_maps(L0_attn_x, L0_attn_w_q, L0_attn_w_k, L0_attn_w_v,
                           L0_ffn_W1, L0_ffn_b1, L0_ffn_W2, L0_ffn_b2,
                           NSEQ=NSEQ, F=F)
    res = run_bass_kernel_spmd(nc, in_maps, core_ids=list(range(N_CORES)))
    outs = [np.asarray(r["out"]) for r in res.results]
    return np.concatenate(outs, axis=0).reshape(16, 2048, 1024)


# version 24 (final)
# speedup vs baseline: 1.0192x; 1.0065x over previous
import numpy as np
import ml_dtypes
from contextlib import ExitStack
from collections import Counter

import concourse.bass as bass
import concourse.mybir as mybir
from concourse.bass_utils import run_bass_kernel_spmd

F32 = mybir.dt.float32
BF16 = mybir.dt.bfloat16
ALU = mybir.AluOpType
ACTF = mybir.ActivationFunctionType
AX = mybir.AxisListType.X

D = 1024
DC = D // 128
EPS_LN = 1e-5
EPS_Q = 1e-5
N_CORES = 8
G0 = ("sV", 0)


class Walk:
    def __init__(self, ename, eng, sems):
        self.ename = ename
        self.eng = eng
        self.sems = sems
        self.c = Counter()

    AUTO = {"vector": "sV", "scalar": "sS"}

    def op(self, target, fn, inc=None, wait=None, is_dma=False):
        auto = None if is_dma else self.AUTO.get(target)
        if auto is not None:
            assert inc is None or inc == (auto, 1), f"bad inc {inc} on {target}"
        if wait:
            for sem, val in wait:
                assert val <= self.c[sem], \
                    f"wait {sem}>={val} ahead of counter {self.c[sem]} ({self.ename})"
                if val > 0 and self.ename == target:
                    self.eng.wait_ge(self.sems[sem], val)
        if self.ename == target:
            inst = fn(self.eng)
            if auto is not None:
                if self.c[auto] > 0:
                    inst._wait_ge(self.sems[auto], self.c[auto])
                inst.then_inc(self.sems[auto], 1)
            elif inc is not None:
                inst.then_inc(self.sems[inc[0]], inc[1])
        if auto is not None:
            self.c[auto] += 1
        elif inc is not None:
            self.c[inc[0]] += inc[1]

    def dma(self, target, out, in_, sem, wait=None):
        self_wait = [(sem, self.c[sem])]
        self.op(target, lambda e: e.dma_start(out=out, in_=in_), inc=(sem, 16),
                wait=self_wait + (wait or []), is_dma=True)

    def g(self, sem):
        return (sem, self.c[sem])


def build_program(T=2048, NSEQ=2, F=4096):
    assert T % 512 == 0 and F % 1024 == 0
    FC = F // 128
    FS = F // 512
    NT = T // 128
    GT = min(1024, T)
    TPG = GT // 128
    NG = T // GT
    SC = T // 512

    nc = bass.Bass("TRN2", target_bir_lowering=False, debug=False)

    x_t = nc.dram_tensor("x", [NSEQ, T, D], F32, kind="ExternalInput")
    w1t_t = nc.dram_tensor("w1t", [D, F], F32, kind="ExternalInput")
    w2t_t = nc.dram_tensor("w2t", [F, D], F32, kind="ExternalInput")
    wq_t = nc.dram_tensor("wq", [D, D], F32, kind="ExternalInput")
    wk_t = nc.dram_tensor("wk", [D, D], F32, kind="ExternalInput")
    wv_t = nc.dram_tensor("wv", [D, D], F32, kind="ExternalInput")
    b1r_t = nc.dram_tensor("b1r", [128, FC], F32, kind="ExternalInput")
    b2f_t = nc.dram_tensor("b2f", [1, D], F32, kind="ExternalInput")
    id_t = nc.dram_tensor("id128", [128, 128], BF16, kind="ExternalInput")
    onesf_t = nc.dram_tensor("onesf", [128, 128], F32, kind="ExternalInput")
    onesb_t = nc.dram_tensor("onesb", [1, 512], BF16, kind="ExternalInput")
    mask_t = nc.dram_tensor("mask", [128, 128], F32, kind="ExternalInput")
    out_t = nc.dram_tensor("out", [NSEQ, T, D], F32, kind="ExternalOutput")
    w1q_t = nc.dram_tensor("w1q", [FS, 128, DC, 512], BF16, kind="Internal")
    w2q_t = nc.dram_tensor("w2q", [4, 128, FC, 256], BF16, kind="Internal")

    x = x_t.ap()
    out = out_t.ap()

    ctx = ExitStack()
    SB = lambda n, sh, dt: ctx.enter_context(nc.sbuf_tensor(n, sh, dt))

    stream = SB("stream", [128, NT, D], F32)
    abig = SB("abig", [128, 16 * 1024], F32)
    abig_bf = abig.bitcast(BF16)
    assert FC * GT <= 32768 and 2 * DC * T <= 32768
    h_sb = abig_bf.reshape([128, 32768 // (FC * GT), FC, GT])
    K_view = abig_bf.reshape([128, 32768 // (DC * T), DC, T])
    V_view = abig_bf.reshape([128, 32768 // (NT * D), NT, D])

    amid = SB("amid", [128, 8 * 1024], F32)
    amid_bf = amid.bitcast(BF16)
    assert DC * T <= 16384 and DC * GT <= 8192
    xlng = amid_bf.reshape([128, 16384 // (DC * GT), DC, GT])
    xln2 = amid_bf.reshape([128, 16384 // (DC * T), DC, T])
    qin = amid.reshape([128, 8, 1024])
    qstg = amid_bf.reshape([128, 16, 1024])
    amid_i32 = amid.bitcast(mybir.dt.int32).reshape([128, 8, 1024])
    qin2a = amid.reshape([128, 4, 2048])
    stg2a = amid_bf.reshape([128, 8, 2048])
    stg2b = amid_bf.reshape([128, 8, 2048])

    aw = SB("aw", [128, 8 * 1024], F32)
    aw_bf = aw.bitcast(BF16)
    w1b = aw_bf.reshape([128, 16384 // (DC * 512), DC, 512])
    WA = 8192 // (FC * 256)
    w2s = aw_bf.reshape([128, 16384 // (FC * 256), FC, 256])
    wkb = aw_bf.reshape([128, 2, DC, D])
    pslot = aw_bf.reshape([128, 8, 2048])
    q_view = aw_bf.reshape([128, 4, DC, 512])
    qin2b = aw.reshape([128, 4, 2048])
    i32t2 = aw.bitcast(mybir.dt.int32).reshape([128, 4, 2048])

    xm = SB("xm", [128, 2, D], BF16)
    sqs = SB("sqs", [128, D], BF16)
    ssum = SB("ssum", [128, NT], F32)
    ssq = SB("ssq", [128, NT], F32)
    mneg = SB("mneg", [128, NT], F32)
    ex2e = SB("ex2e", [128, NT], F32)
    m2v = SB("m2v", [128, NT], F32)
    vpe = SB("vpe", [128, NT], F32)
    rcv = SB("rcv", [128, NT], F32)
    rinv = SB("rinv", [128, NT], F32)
    nmr = SB("nmr", [128, NT], F32)
    gp = SB("gp", [128, 32], F32)
    gtot = SB("gtot", [128, 2], F32)
    invg = SB("invg", [128, 2], F32)
    id_sb = SB("id_sb", [128, 128], BF16)
    onesf_sb = SB("onesf_sb", [128, 128], F32)
    onesb_sb = SB("onesb_sb", [1, 512], BF16)
    mask_sb = SB("mask_sb", [128, 128], F32)
    b1r_sb = SB("b1r_sb", [128, FC], F32)
    b2bf = SB("b2bf", [1, D], BF16)
    cmax = SB("cmax", [128, 4], F32)
    negb = SB("negb", [128, 1], F32)
    rs_sb = SB("rs_sb", [128, 4], F32)
    rsum = SB("rsum", [128, 1], F32)
    rinvp = SB("rinvp", [128, 2], F32)
    pbuf2 = SB("pbuf2", [128, 2048], BF16)

    PS = lambda n: ctx.enter_context(nc.psum_tensor(n, [128, 512], F32))
    pA = [PS("pA0"), PS("pA1"), PS("pA2"), PS("pA3")]
    pY = [PS("pY0"), PS("pY1")]
    pT = [PS("pT0"), PS("pT1")]

    sems = {n: ctx.enter_context(nc.semaphore(n))
            for n in ("dG", "sT", "sV", "sS", "dC", "dQa", "dQb", "dQc", "dQd",
                      "dQe", "dQf", "dQg", "dQh", "dSa", "dSb", "dSc", "dSd",
                      "dX", "dW1a", "dW1b", "dW2a", "dW2b", "dO")}

    def walk(w: Walk):
        c = w.c
        gates = {
            "xm": [G0, G0],
            "pt": [G0, G0],
            "pa": [G0, G0, G0, G0],
            "py": [G0, G0],
            "w1": [G0, G0],
            "w2": [G0, G0],
            "h": [G0, G0],
            "aw": [G0],
            "store": [G0, G0],
            "p2v": [G0],
            "w1q": [[G0]] * FS,
            "w2q": [[G0]] * 4,
            "prevexp": [G0],
            "qext": [],
            "xpre": [],
            "aw2": [],
            "prew1": {},
            "premarks": None,
        }

        for dst, src in ((id_sb[:], id_t.ap()), (onesf_sb[:], onesf_t.ap()),
                         (onesb_sb[:], onesb_t.ap()), (mask_sb[:], mask_t.ap()),
                         (b1r_sb[:], b1r_t.ap()), (qin2a[0:1, 2, 0:D], b2f_t.ap())):
            w.dma("sync", dst, src, "dC")
        w.op("vector", lambda e: e.tensor_copy(b2bf[:], qin2a[0:1, 2, 0:D]),
             inc=("sV", 1), wait=[w.g("dC")])

        QSEM = ["dQa", "dQb", "dQc", "dQd"]
        QBUF = [qin2a[:, 2, :], qin2b[:, 2, :]]
        NQB = 2
        if NT >= TPG + 4:
            NQB = 4
            QBUF.append(stream[:, TPG:TPG + 2, :].rearrange("p t d -> p (t d)"))
            QBUF.append(stream[:, TPG + 2:TPG + 4, :].rearrange("p t d -> p (t d)"))
        STG = [stg2a[:, 6, :], stg2b[:, 7, :]]
        qin_free = [("sV", 1), G0, G0, G0]
        SSEM = ["dSa", "dSb", "dSc", "dSd"]
        pend_writes = []
        NW2R = 8
        NFQ2 = F // 1024
        w1mark = [G0] * FS
        w2mark = [G0] * 4
        w1slot_free = [[G0], [G0]]
        w2slot_free = [[G0], [G0]]
        stage1_done = [G0] * (FS + 1)
        stage2_done = [G0] * 5
        ssem_i = [0]

        def flush_write():
            if pend_writes:
                dst, ssrc, sv, tag = pend_writes.pop(0)
                si = ssem_i[0] % 4
                ssem_i[0] += 1
                w.dma("sync", dst, ssrc, SSEM[si], wait=[sv])
                if tag is not None:
                    done = [w.g(s) for s in SSEM]
                    kind, idx = tag
                    if kind == 0:
                        gates["w1q"][idx] = done
                        w1slot_free[idx % 2] = done
                    else:
                        gates["w2q"][idx] = done
                        w2slot_free[idx % 2] = done

        def q_load(src_ap, bi, k, extra=None):
            w.dma("sync", QBUF[bi].rearrange("p (c s) -> p c s", s=256),
                  src_ap, QSEM[bi], wait=[qin_free[bi]] + (extra or []))
            return w.g(QSEM[bi])

        def w1_src(fs, h):
            return w1t_t.ap()[:, fs * 512 + h * 256:fs * 512 + (h + 1) * 256]\
                .rearrange("(c p) s -> p c s", p=128)

        def w2_src(dq, fq):
            return w2t_t.ap()[fq * 1024:(fq + 1) * 1024,
                              dq * 256:(dq + 1) * 256]\
                .rearrange("(c p) s -> p c s", p=128)

        gcount = [0]

        def gamma_chunk(mat, src_ap, bi, extra=None):
            qv = q_load(src_ap, bi, 0)
            k = gcount[0]
            gcount[0] += 1
            ws = [qv] + ([extra] if extra is not None else [])
            w.op("scalar", lambda e, k=k, bi=bi: e.activation(
                sqs[:], QBUF[bi][:, 0:1024], ACTF.Abs,
                accum_out=gp[:, 2 * k:2 * k + 1]), wait=ws)
            w.op("scalar", lambda e, k=k, bi=bi: e.activation(
                sqs[:], QBUF[bi][:, 1024:2048], ACTF.Abs,
                accum_out=gp[:, 2 * k + 1:2 * k + 2]))
            qin_free[bi] = w.g("sS")

        def gamma_finish(mat):
            nk = 2 * gcount[0]
            gcount[0] = 0
            w.op("tensor", lambda e, nk=nk: e.matmul(
                pT[0][:, 0:nk], onesf_sb[:], gp[:, 0:nk], start=True, stop=True),
                inc=("sT", 1), wait=[w.g("sS"), gates["pt"][0]])
            w.op("vector", lambda e, mat=mat, nk=nk: e.reduce_sum(
                gtot[:, mat:mat + 1], pT[0][:, 0:nk], axis=AX), wait=[w.g("sT")])
            w.op("vector", lambda e, mat=mat: e.tensor_scalar(
                gtot[:, mat:mat + 1], gtot[:, mat:mat + 1], 1.0 / (D * F), EPS_Q,
                ALU.mult, ALU.add))
            w.op("vector", lambda e, mat=mat: e.reciprocal(
                invg[:, mat:mat + 1], gtot[:, mat:mat + 1]))
            gates["pt"][0] = w.g("sV")

        def quant_passes(mat, bi, si, qv, target, tgate):
            ig = invg[:, mat:mat + 1]
            sa, sb = si % 2, (si + 1) % 2
            w.op("vector", lambda e, bi=bi, sa=sa, ig=ig: e.tensor_scalar(
                STG[sa], QBUF[bi], ig, 0.5, ALU.mult, ALU.is_ge), wait=[qv])
            w.op("vector", lambda e, bi=bi, sb=sb, ig=ig: e.tensor_scalar(
                STG[sb], QBUF[bi], ig, -0.5, ALU.mult, ALU.is_le))
            qin_free[bi] = w.g("sV")
            shp = target
            w.op("vector", lambda e, sa=sa, sb=sb, shp=shp: e.tensor_tensor(
                shp, STG[sa].rearrange("p (c s) -> p c s", s=256),
                STG[sb].rearrange("p (c s) -> p c s", s=256), ALU.subtract),
                wait=tgate)
            return w.g("sV")

        def quant_w1_slice(fs):
            sl = fs % 2
            for h in range(2):
                bi = (2 * fs + h) % NQB
                qv = q_load(w1_src(fs, h), bi, 0)
                target = w1b[:, sl, :, h * 256:(h + 1) * 256]
                tg = list(w1slot_free[sl]) + [stage1_done[max(0, fs - 1)]]
                sv = quant_passes(0, bi, (2 * fs + h) % 2, qv, target, tg)
                dst = w1q_t.ap()[fs, :, :, h * 256:(h + 1) * 256]
                pend_writes.append((dst, target, sv,
                                    (0, fs) if h == 1 else None))
            w1mark[fs] = w.g("sV")
            while len(pend_writes) > 2:
                flush_write()

        def quant_w2_unit(dq, gate_extra, late=False):
            while pend_writes:
                flush_write()
            sl = dq % 2
            slot = w2s[:, WA] if sl == 0 else w2s[:, 0]
            for fq in range(NFQ2):
                if late:
                    bi = 0
                else:
                    bi = [0, 2, 3][fq % 3] if NQB == 4 else 0
                qv = q_load(w2_src(dq, fq), bi, 0)
                target = slot[:, fq * NW2R:(fq + 1) * NW2R, :]
                tg = list(w2slot_free[sl]) + (gate_extra if fq == 0 else [])
                sv = quant_passes(1, bi, fq % 2, qv, target, tg)
                dst = w2q_t.ap()[dq, :, fq * NW2R:(fq + 1) * NW2R, :]
                pend_writes.append((dst, target, sv,
                                    (1, dq) if fq == NFQ2 - 1 else None))
            w2mark[dq] = w.g("sV")
            while len(pend_writes) > 2:
                flush_write()

        qin2b_last = [G0]

        def quant_pre():
            for fs in range(FS):
                for h in range(2):
                    gamma_chunk(0, w1_src(fs, h), (2 * fs + h) % NQB)
            gamma_finish(0)
            g1mm = w.g("sT")
            for dq in range(4):
                for fq in range(NFQ2):
                    gamma_chunk(1, w2_src(dq, fq), (4 * dq + fq) % NQB,
                                extra=g1mm if dq == 0 and fq == 0 else None)
            quant_w1_slice(0)
            quant_w1_slice(1)

        def quant_fs_hook(fs):
            if 1 <= fs <= FS - 2:
                quant_w1_slice(fs + 1)
            if fs == FS - 2:
                gamma_finish(1)
            if fs == FS:
                while pend_writes:
                    flush_write()
                alldS = [w.g(s) for s in SSEM]
                ga = [qin_free[1], ("dQb", w.c["dQb"]), ("sS", w.c["sS"])]
                quant_w2_unit(0, ga)
                quant_w2_unit(1, [stage1_done[FS]] + alldS)

        def quant_dq_hook(dq):
            if dq == 1:
                quant_w2_unit(2, [stage2_done[0]], late=True)
            elif dq == 2:
                quant_w2_unit(3, [stage2_done[1]], late=True)
            elif dq == 4:
                while pend_writes:
                    flush_write()
                qdone = [w.g(s) for s in SSEM] + [w.g("sV")]
                gates["w2q"] = [qdone] * 4
                gates["qext"] = [w.g("sV"), w.g("sS")]
                gates["aw2"] = qdone

        def ln_stats(tiles, v_waits, s_waits):
            ntl = len(tiles)
            t0 = tiles[0]
            for i, t in enumerate(tiles):
                st_ap = stream[:, t, :]
                w.op("vector", lambda e, st_ap=st_ap, t=t: e.reduce_sum(
                    ssum[:, t:t + 1], st_ap, axis=AX, negate=True),
                    wait=(v_waits if i == 0 else None))
                w.op("scalar", lambda e, st_ap=st_ap, t=t: e.activation(
                    sqs[:], st_ap, ACTF.Square, accum_out=ssq[:, t:t + 1]),
                    wait=(s_waits if i == 0 else None))
            sq_ss = w.g("sS")
            w.op("vector", lambda e, t0=t0, ntl=ntl: e.tensor_scalar(
                mneg[:, t0:t0 + ntl], ssum[:, t0:t0 + ntl], 1.0 / D, None, ALU.mult),
                wait=[sq_ss])
            w.op("vector", lambda e, t0=t0, ntl=ntl: e.tensor_scalar(
                ex2e[:, t0:t0 + ntl], ssq[:, t0:t0 + ntl], 1.0 / D, EPS_LN,
                ALU.mult, ALU.add))
            w.op("vector", lambda e, t0=t0, ntl=ntl: e.tensor_tensor(
                m2v[:, t0:t0 + ntl], mneg[:, t0:t0 + ntl], mneg[:, t0:t0 + ntl],
                ALU.mult))
            w.op("vector", lambda e, t0=t0, ntl=ntl: e.tensor_tensor(
                vpe[:, t0:t0 + ntl], ex2e[:, t0:t0 + ntl], m2v[:, t0:t0 + ntl],
                ALU.subtract))
            w.op("vector", lambda e, t0=t0, ntl=ntl: e.reciprocal(
                rcv[:, t0:t0 + ntl], vpe[:, t0:t0 + ntl]))
            rc_sv = w.g("sV")
            w.op("scalar", lambda e, t0=t0, ntl=ntl: e.activation(
                rinv[:, t0:t0 + ntl], rcv[:, t0:t0 + ntl], ACTF.Sqrt),
                wait=[rc_sv])
            ri_ss = w.g("sS")
            w.op("vector", lambda e, t0=t0, ntl=ntl: e.tensor_tensor(
                nmr[:, t0:t0 + ntl], mneg[:, t0:t0 + ntl], rinv[:, t0:t0 + ntl],
                ALU.mult), wait=[ri_ss])

        def ln_tr(tiles, dest_fn, qwait=None):
            marks = []
            for i, t in enumerate(tiles):
                st_ap = stream[:, t, :]
                xs = i % 2
                w.op("vector", lambda e, st_ap=st_ap, t=t, xs=xs: e.tensor_scalar(
                    xm[:, xs, :], st_ap, rinv[:, t:t + 1], nmr[:, t:t + 1],
                    ALU.mult, ALU.add), wait=[gates["xm"][xs]])
                xm_v = w.g("sV")
                for cb in range(2):
                    if i % 2 == 0:
                        bank = pT[cb]
                        glist, gi = gates["pt"], cb
                    else:
                        bank = pA[cb]
                        glist, gi = gates["pa"], cb
                    for cc in range(4):
                        ch = cb * 4 + cc
                        w.op("tensor", lambda e, xs=xs, ch=ch, cc=cc, bank=bank: e.matmul(
                            bank[:, cc * 128:(cc + 1) * 128],
                            xm[:, xs, ch * 128:(ch + 1) * 128], id_sb[:],
                            start=True, stop=True),
                            inc=("sT", 1) if cc == 3 else None,
                            wait=[xm_v, glist[gi]] if cc == 0 else None)
                    dst = dest_fn(i, t, cb)
                    cw = [w.g("sT")] + (qwait if (i == 0 and cb == 0 and qwait) else [])
                    w.op("scalar", lambda e, dst=dst, bank=bank: e.copy(dst, bank[:]),
                         inc=("sS", 1), wait=cw)
                    glist[gi] = w.g("sS")
                gates["xm"][xs] = w.g("sT")
                marks.append((w.g("sV"), w.g("sS")))
            return marks

        def load_x_group(s, g):
            src = x[s, g * GT:(g + 1) * GT, :].rearrange("(t p) d -> p t d", p=128)
            extra = list(gates["qext"]) if g >= 1 else []
            if s == 0 and g == 1 and NQB == 4 and not gates["qext"]:
                extra = [qin_free[2], qin_free[3],
                         ("dQc", w.c["dQc"]), ("dQd", w.c["dQd"])]
            w.dma("sync", stream[:, g * TPG:(g + 1) * TPG, :], src, "dX",
                  wait=[gates["store"][g % 2]] + extra)
            return w.g("dX")

        def ffn_dest(i, t, cb):
            tl = t % TPG
            return xlng[:, 0, cb * 4:cb * 4 + 4, tl * 128:(tl + 1) * 128]

        def ffn_phase(s, is_p1, pre_marks=None, fs_hook=None, dq_hook=None,
                      tr_hook=None):
            for g in range(NG):
                tiles = list(range(g * TPG, (g + 1) * TPG))
                if g == 0 and pre_marks is None and is_p1:
                    if gates.get("xpre"):
                        xw = gates["xpre"][0]
                        gates["xpre"] = []
                    else:
                        xw = load_x_group(s, 0)
                    ln_stats(tiles, [xw], [xw, ("sV", 1)])
                if g == 0 and pre_marks is not None:
                    marks = pre_marks
                else:
                    marks = ln_tr(tiles, ffn_dest)
                xln_rdy = marks[-1]

                NH = GT // 512
                startup = (fs_hook is not None and g == 0)
                for fs in range(FS):
                    sl = fs % 2
                    if startup:
                        fs_hook(fs)
                        wv = w1mark[fs]
                    else:
                        if g == 0 and fs in gates["prew1"]:
                            wv = gates["prew1"].pop(fs)
                        else:
                            wsem = "dW1a" if sl == 0 else "dW1b"
                            w.dma("sync", w1b[:, sl], w1q_t.ap()[fs], wsem,
                                  wait=[gates["w1"][sl]] + gates["w1q"][fs]
                                  + gates["aw2"])
                            wv = w.g(wsem)
                    for fj in range(4):
                        fc = fs * 4 + fj
                        par = fc % 3
                        if par == 0:
                            banks, glist, g0i = (pA[0], pA[1]), gates["pa"], 0
                        elif par == 1:
                            banks, glist, g0i = (pA[2], pA[3]), gates["pa"], 2
                        else:
                            banks, glist, g0i = (pT[0], pT[1]), gates["pt"], 0
                        for ci in range(DC):
                            stat = w1b[:, sl, ci, fj * 128:(fj + 1) * 128]
                            for hf in range(NH):
                                bk = banks[hf]
                                w.op("tensor", lambda e, stat=stat, bk=bk, ci=ci, hf=hf: e.matmul(
                                    bk[:], stat, xlng[:, 0, ci, hf * 512:(hf + 1) * 512],
                                    start=(ci == 0), stop=(ci == DC - 1)),
                                    wait=[wv, xln_rdy[0], xln_rdy[1],
                                          glist[g0i], glist[g0i + 1]]
                                    if (ci == 0 and hf == 0) else None,
                                    inc=("sT", 1) if ci == DC - 1 else None)
                        stv = w.g("sT")
                        if startup:
                            use_v = False
                        else:
                            use_v = (fc % 2 == 0)
                        eng = "vector" if use_v else "scalar"
                        sem = "sV" if use_v else "sS"
                        for hf in range(NH):
                            bk = banks[hf]
                            if eng == "vector":
                                w.op(eng, lambda e, fc=fc, hf=hf, bk=bk: e.tensor_scalar(
                                    h_sb[:, 0, fc, hf * 512:(hf + 1) * 512], bk[:],
                                    b1r_sb[:, fc:fc + 1], 0.0, ALU.add, ALU.max),
                                    inc=(sem, 1), wait=[stv] if hf == 0 else None)
                            else:
                                w.op(eng, lambda e, fc=fc, hf=hf, bk=bk: e.activation(
                                    h_sb[:, 0, fc, hf * 512:(hf + 1) * 512], bk[:],
                                    ACTF.Relu, bias=b1r_sb[:, fc:fc + 1]),
                                    inc=(sem, 1), wait=[stv] if hf == 0 else None)
                        glist[g0i] = w.g(sem)
                        glist[g0i + 1] = w.g(sem)
                        gates["h"][fc % 2] = w.g(sem)
                    gates["w1"][sl] = w.g("sT")
                    if startup:
                        stage1_done[fs + 1] = ("sT", w.c["sT"])
                if startup:
                    fs_hook(FS)
                h_rdy = (gates["h"][0], gates["h"][1])

                if g + 1 < NG:
                    ntiles = list(range((g + 1) * TPG, (g + 2) * TPG))
                    if is_p1:
                        xw2 = load_x_group(s, g + 1)
                        ln_stats(ntiles, [xw2], [xw2])
                    else:
                        ln_stats(ntiles, None, None)
                if tr_hook is not None and g == NG - 1:
                    tr_hook()

                for dq in range(4):
                    buf = w2s[:, WA] if dq % 2 == 0 else w2s[:, 0]
                    if startup:
                        dq_hook(dq)
                        wv = w2mark[dq]
                    else:
                        if dq % 2 == 0:
                            bg = [gates["w2"][0]]
                        else:
                            bg = [gates["w2"][1], gates["w1"][0], gates["w1"][1]]
                        wsem = "dW2a" if dq % 2 == 0 else "dW2b"
                        w.dma("sync", buf, w2q_t.ap()[dq], wsem,
                              wait=bg + gates["w2q"][dq])
                        wv = w.g(wsem)
                    for tb in range(TPG):
                        bank = pY[tb % 2]
                        t_glob = g * TPG + tb
                        for fc in range(FC):
                            stat = h_sb[:, 0, fc, tb * 128:(tb + 1) * 128]
                            w.op("tensor", lambda e, stat=stat, bank=bank, buf=buf, fc=fc: e.matmul(
                                bank[:, 0:256], stat, buf[:, fc, :],
                                start=(fc == 0), stop=False),
                                wait=[wv, h_rdy[0], h_rdy[1], gates["py"][tb % 2]]
                                if fc == 0 else None)
                        w.op("tensor", lambda e, bank=bank, dq=dq: e.matmul(
                            bank[:, 0:256], onesb_sb[0:1, 0:128],
                            b2bf[0:1, dq * 256:(dq + 1) * 256], start=False, stop=True),
                            inc=("sT", 1))
                        w.op("vector",
                             lambda e, bank=bank, t_glob=t_glob, dq=dq: e.scalar_tensor_tensor(
                                 stream[:, t_glob, dq * 256:(dq + 1) * 256], bank[:, 0:256],
                                 0.5, stream[:, t_glob, dq * 256:(dq + 1) * 256],
                                 ALU.mult, ALU.add),
                             inc=("sV", 1), wait=[w.g("sT")])
                        gates["py"][tb % 2] = w.g("sV")
                    gates["w2"][dq % 2] = w.g("sT")
                    if startup:
                        stage2_done[dq + 1] = ("sT", w.c["sT"])
                if startup:
                    dq_hook(4)
                gates["aw"][0] = w.g("sT")
                gates["w1"] = [w.g("sT"), w.g("sT")]
                if not is_p1 and g == NG - 1 and s + 1 < NSEQ:
                    for pfs in range(2):
                        wsem = "dW1a" if pfs == 0 else "dW1b"
                        w.dma("sync", w1b[:, pfs], w1q_t.ap()[pfs], wsem,
                              wait=[gates["w1"][pfs]] + gates["w1q"][pfs]
                              + gates["aw2"])
                        gates["prew1"][pfs] = w.g(wsem)

                if is_p1:
                    ln_stats(tiles, [("sV", c["sV"])], [("sV", c["sV"])])
                if not is_p1:
                    dst = out[s, g * GT:(g + 1) * GT, :].rearrange("(t p) d -> p t d", p=128)
                    w.dma("sync", dst, stream[:, g * TPG:(g + 1) * TPG, :], "dO",
                          wait=[w.g("sV")])
                    gates["store"][g % 2] = w.g("dO")
                    if g == 0 and s + 1 < NSEQ:
                        gates["xpre"] = [load_x_group(s + 1, 0)]
                    if g == NG - 1 and s + 1 < NSEQ:
                        xw3 = gates["xpre"][0] if gates["xpre"] else G0
                        gates["xpre"] = []
                        ln_stats(list(range(TPG)), [xw3], [xw3])
                        gates["premarks"] = ln_tr(list(range(TPG)), ffn_dest)

        def p2_dest(i, t, cb):
            return xln2[:, 0, cb * 4:cb * 4 + 4, t * 128:(t + 1) * 128]

        p2pre = {"marks": None}

        def p2_tr_hook():
            p2pre["marks"] = ln_tr(list(range(TPG)), p2_dest,
                                   qwait=list(gates["aw2"]))

        def attn_phase(s):
            pm = p2pre["marks"] if p2pre["marks"] is not None else []
            p2pre["marks"] = None
            p2marks = pm + ln_tr(list(range(len(pm), NT)), p2_dest,
                                 qwait=list(gates["aw2"]))

            w.dma("gpsimd", wkb[:, 0], wk_t.ap().rearrange("(c p) d -> p c d", p=128),
                  "dG", wait=[gates["aw"][0]] + gates["aw2"])
            kv_dg = w.g("dG")
            w.dma("gpsimd", wkb[:, 1], wv_t.ap().rearrange("(c p) d -> p c d", p=128),
                  "dG", wait=[gates["aw"][0]] + gates["aw2"])
            wv_dg = w.g("dG")

            for co in range(DC):
                for ch in range(SC):
                    bi = (co * SC + ch) % 4
                    bank = pA[bi]
                    mk = p2marks[min(ch * 4 + 3, NT - 1)]
                    for ci in range(DC):
                        w.op("tensor", lambda e, co=co, ch=ch, ci=ci, bank=bank: e.matmul(
                            bank[:], wkb[:, 0, ci, co * 128:(co + 1) * 128],
                            xln2[:, 0, ci, ch * 512:(ch + 1) * 512],
                            start=(ci == 0), stop=(ci == DC - 1)),
                            wait=[kv_dg, mk[0], mk[1],
                                  gates["pa"][bi]] if ci == 0 else None,
                            inc=("sT", 1) if ci == DC - 1 else None)
                    w.op("vector", lambda e, co=co, ch=ch, bank=bank: e.tensor_copy(
                        K_view[:, 0, co, ch * 512:(ch + 1) * 512], bank[:]),
                        inc=("sV", 1), wait=[w.g("sT")])
                    gates["pa"][bi] = w.g("sV")

            w.dma("gpsimd", wkb[:, 0], wq_t.ap().rearrange("(c p) d -> p c d", p=128),
                  "dG", wait=[w.g("sT")])
            wq_dg = w.g("dG")

            for tb in range(NT):
                for dh in range(2):
                    bi = (tb * 2 + dh) % 4
                    bank = pA[bi]
                    mk = p2marks[tb]
                    for ci in range(DC):
                        w.op("tensor", lambda e, tb=tb, dh=dh, ci=ci, bank=bank: e.matmul(
                            bank[:], xln2[:, 0, ci, tb * 128:(tb + 1) * 128],
                            wkb[:, 1, ci, dh * 512:(dh + 1) * 512],
                            start=(ci == 0), stop=(ci == DC - 1)),
                            wait=[gates["pa"][bi], wv_dg, mk[0], mk[1]]
                            if ci == 0 else None,
                            inc=("sT", 1) if ci == DC - 1 else None)
                    w.op("scalar", lambda e, tb=tb, dh=dh, bank=bank: e.copy(
                        V_view[:, 1, tb, dh * 512:(dh + 1) * 512], bank[:]),
                        inc=("sS", 1), wait=[w.g("sT")])
                    gates["pa"][bi] = w.g("sS")

            tile_st = {}

            def emit_scores(qt):
                qc = qt // 4
                if qt % 4 == 0:
                    for co in range(DC):
                        bi = co % 4
                        bank = pA[bi]
                        for ci in range(DC):
                            w.op("tensor", lambda e, co=co, ci=ci, qc=qc, bank=bank: e.matmul(
                                bank[:], wkb[:, 0, ci, co * 128:(co + 1) * 128],
                                xln2[:, 0, ci, qc * 512:(qc + 1) * 512],
                                start=(ci == 0), stop=(ci == DC - 1)),
                                wait=[wq_dg, gates["pa"][bi]] if ci == 0 else None,
                                inc=("sT", 1) if ci == DC - 1 else None)
                        w.op("vector", lambda e, co=co, bank=bank: e.tensor_copy(
                            q_view[:, 3, co, :], bank[:]),
                            inc=("sV", 1), wait=[w.g("sT")])
                        gates["pa"][bi] = w.g("sV")
                    if qt == NT - 4:
                        lastq = w.g("sT")
                        for pfs in range(2):
                            wsem = "dW1a" if pfs == 0 else "dW1b"
                            w.dma("sync", w1b[:, pfs], w1q_t.ap()[pfs], wsem,
                                  wait=[lastq, gates["w1"][pfs]]
                                  + gates["w1q"][pfs] + gates["aw2"])
                            gates["prew1"][pfs] = w.g(wsem)
                q_sv = w.g("sV")
                L = (qt + 1) * 128
                nch = (L + 511) // 512
                order = [nch - 1] + list(range(nch - 1))
                for ch in order:
                    W_ = min(512, L - ch * 512)
                    bank = pA[ch]
                    for ci in range(DC):
                        w.op("tensor", lambda e, ch=ch, ci=ci, W_=W_, qt=qt, bank=bank: e.matmul(
                            bank[:, 0:W_],
                            q_view[:, 3, ci, (qt % 4) * 128:(qt % 4 + 1) * 128],
                            K_view[:, 0, ci, ch * 512:ch * 512 + W_],
                            start=(ci == 0), stop=(ci == DC - 1)),
                            wait=[q_sv, gates["pa"][ch]] if ci == 0 else None,
                            inc=("sT", 1) if ci == DC - 1 else None)
                    if ch == nch - 1:
                        lw = L - ch * 512
                        w.op("vector", lambda e, ch=ch, lw=lw: e.tensor_tensor(
                            pA[ch][:, lw - 128:lw], pA[ch][:, lw - 128:lw],
                            mask_sb[:], ALU.add),
                            wait=[w.g("sT"), gates["prevexp"][0]])
                    W2_ = min(512, L - ch * 512)
                    w.op("vector", lambda e, ch=ch, W2_=W2_: e.reduce_max(
                        cmax[:, ch:ch + 1], pA[ch][:, 0:W2_], axis=AX),
                        wait=[w.g("sT"), gates["prevexp"][0]] if ch != nch - 1 else None)
                w.op("vector", lambda e, nch=nch: e.reduce_max(
                    negb[:], cmax[:, 0:nch], axis=AX, negate=True))
                w.op("vector", lambda e: e.tensor_scalar(
                    negb[:], negb[:], 1.0 / 32.0, None, ALU.mult), inc=("sV", 1))
                mx_sv = w.g("sV")
                pb = pslot[:, 4, :] if qt % 2 == 0 else pbuf2
                exp_marks = []
                for ch in range(nch):
                    W_ = min(512, L - ch * 512)
                    w.op("scalar", lambda e, ch=ch, W_=W_, pb=pb: e.activation(
                        pb[ch * 512:ch * 512 + W_] if False else pb[:, ch * 512:ch * 512 + W_],
                        pA[ch][:, 0:W_], ACTF.Exp,
                        bias=negb[:, 0:1], scale=1.0 / 32.0,
                        accum_out=rs_sb[:, ch:ch + 1]),
                        inc=("sS", 1), wait=[mx_sv] if ch == 0 else None)
                    gates["pa"][ch] = w.g("sS")
                    exp_marks.append(w.g("sS"))
                gates["prevexp"][0] = w.g("sS")
                w.op("vector", lambda e, nch=nch: e.reduce_sum(
                    rsum[:], rs_sb[:, 0:nch], axis=AX), wait=[w.g("sS")])
                w.op("vector", lambda e, qt=qt: e.reciprocal(
                    rinvp[:, qt % 2:qt % 2 + 1], rsum[:]), inc=("sV", 1))
                tile_st[qt] = (exp_marks, w.g("sV"), pb)

            def emit_av(qt):
                exp_marks, ri_sv, pb = tile_st.pop(qt)
                nj = qt + 1
                for jb in range((nj + 3) // 4):
                    bank = pT[jb % 2]
                    njb = min(4, nj - jb * 4)
                    for jj in range(njb):
                        j = jb * 4 + jj
                        w.op("tensor", lambda e, j=j, jj=jj, bank=bank, pb=pb: e.matmul(
                            bank[:, jj * 128:(jj + 1) * 128],
                            pb[:, j * 128:(j + 1) * 128], id_sb[:],
                            start=True, stop=True),
                            wait=[exp_marks[jb], gates["pt"][jb % 2]] if jj == 0 else None,
                            inc=("sT", 1) if jj == njb - 1 else None)
                    w.op("scalar", lambda e, jb=jb, njb=njb, bank=bank: e.copy(
                        pslot[:, 5, jb * 512:jb * 512 + njb * 128], bank[:, 0:njb * 128]),
                        inc=("sS", 1), wait=[w.g("sT")])
                    gates["pt"][jb % 2] = w.g("sS")
                    aT_ss = w.g("sS")
                    for jj in range(njb):
                        j = jb * 4 + jj
                        for dh in range(2):
                            w.op("tensor", lambda e, j=j, dh=dh, qt=qt: e.matmul(
                                pY[dh][:], pslot[:, 5, j * 128:(j + 1) * 128],
                                V_view[:, 1, j, dh * 512:(dh + 1) * 512],
                                start=(j == 0), stop=(j == qt)),
                                wait=[aT_ss, gates["py"][0], gates["py"][1]]
                                if (jj == 0 and dh == 0) else None,
                                inc=("sT", 1) if (j == qt and dh == 1) else None)
                yt = w.g("sT")
                for dh in range(2):
                    w.op("vector", lambda e, dh=dh, qt=qt: e.scalar_tensor_tensor(
                        stream[:, qt, dh * 512:(dh + 1) * 512], pY[dh][:],
                        rinvp[:, qt % 2:qt % 2 + 1],
                        stream[:, qt, dh * 512:(dh + 1) * 512],
                        ALU.mult, ALU.add),
                        inc=("sV", 1), wait=[yt, ri_sv] if dh == 0 else None)
                    gates["py"][dh] = w.g("sV")
                if qt == TPG - 1:
                    gv = w.g("sV")
                    ln_stats(list(range(TPG)), [gv], [gv])

            for qt in range(NT):
                emit_scores(qt)
                if qt > 0:
                    emit_av(qt - 1)
            emit_av(NT - 1)
            gates["p2v"][0] = w.g("sV")
            gates["aw"][0] = w.g("sT")
            gates["w1"] = [w.g("sT"), w.g("sT")]
            gates["w2"] = [w.g("sT"), w.g("sT")]

        xw0 = load_x_group(0, 0)
        ln_stats(list(range(TPG)), [xw0], [xw0, ("sV", 1)])
        pre_marks = ln_tr(list(range(TPG)), ffn_dest)
        quant_pre()
        for s in range(NSEQ):
            if s == 0:
                pm = pre_marks
            else:
                pm = gates["premarks"]
                gates["premarks"] = None
            ffn_phase(s, True, pre_marks=pm,
                      fs_hook=quant_fs_hook if s == 0 else None,
                      dq_hook=quant_dq_hook if s == 0 else None,
                      tr_hook=p2_tr_hook if NG > 1 else None)
            attn_phase(s)
            ffn_phase(s, False)

        w.op("sync", lambda e: e.wait_ge(sems["dO"], c["dO"]))
        w.op("gpsimd", lambda e: e.wait_ge(sems["dG"], c["dG"]))

    with nc.Block() as block:
        for ename, dec in (("sync", block.sync), ("gpsimd", block.gpsimd),
                           ("vector", block.vector), ("scalar", block.scalar),
                           ("tensor", block.tensor)):
            def mk(en):
                def f(eng):
                    walk(Walk(en, eng, sems))
                return f
            dec(mk(ename))
    ctx.close()
    return nc


def make_in_maps(L0_attn_x, L0_attn_w_q, L0_attn_w_k, L0_attn_w_v,
                 L0_ffn_W1, L0_ffn_b1, L0_ffn_W2, L0_ffn_b2,
                 NSEQ=2, F=4096, n_cores=N_CORES):
    FC = F // 128
    id128 = np.eye(128, dtype=ml_dtypes.bfloat16)
    onesf = np.ones((128, 128), np.float32)
    onesb = np.ones((1, 512), ml_dtypes.bfloat16)
    mask = np.where(np.arange(128)[None, :] <= np.arange(128)[:, None],
                    np.float32(0.0), np.float32(-1e30)).astype(np.float32)
    w1t = np.ascontiguousarray(np.asarray(L0_ffn_W1, np.float32).T)
    w2t = np.ascontiguousarray(np.asarray(L0_ffn_W2, np.float32).T)
    b1r = np.ascontiguousarray(np.asarray(L0_ffn_b1, np.float32).reshape(FC, 128).T)
    b2f = np.ascontiguousarray(np.asarray(L0_ffn_b2, np.float32).reshape(1, D))
    xfull = np.asarray(L0_attn_x, np.float32)
    in_maps = []
    for core in range(n_cores):
        xs = np.ascontiguousarray(xfull[core * NSEQ:(core + 1) * NSEQ])
        in_maps.append({
            "x": xs, "w1t": w1t, "w2t": w2t,
            "wq": np.ascontiguousarray(np.asarray(L0_attn_w_q, np.float32)),
            "wk": np.ascontiguousarray(np.asarray(L0_attn_w_k, np.float32)),
            "wv": np.ascontiguousarray(np.asarray(L0_attn_w_v, np.float32)),
            "b1r": b1r, "b2f": b2f,
            "id128": id128, "onesf": onesf, "onesb": onesb, "mask": mask,
        })
    return in_maps


def kernel(L0_attn_x, L0_attn_w_q, L0_attn_w_k, L0_attn_w_v,
           L0_ffn_W1, L0_ffn_b1, L0_ffn_W2, L0_ffn_b2):
    T, NSEQ, F = 2048, 2, 4096
    nc = build_program(T=T, NSEQ=NSEQ, F=F)
    in_maps = make_in_maps(L0_attn_x, L0_attn_w_q, L0_attn_w_k, L0_attn_w_v,
                           L0_ffn_W1, L0_ffn_b1, L0_ffn_W2, L0_ffn_b2,
                           NSEQ=NSEQ, F=F)
    res = run_bass_kernel_spmd(nc, in_maps, core_ids=list(range(N_CORES)))
    outs = [np.asarray(r["out"]) for r in res.results]
    return np.concatenate(outs, axis=0).reshape(16, 2048, 1024)
